# version 03
# baseline (speedup 1.0000x reference)
import math
import numpy as np
import ml_dtypes

import concourse.bass as bass
import concourse.mybir as mybir
from concourse.bass_utils import run_bass_kernel_spmd

F32 = mybir.dt.float32
BF16 = mybir.dt.bfloat16
AX = mybir.AxisListType
ALU = mybir.AluOpType
ACTF = mybir.ActivationFunctionType

T = 2048
D = 1024
NT = 16
NC = 8
HG = 16
HL = 8
INNER = 512
HID = 2730
W1C = 22
LAST = 2730 - 21 * 128
B0 = 40.0
STAGE = 99
USE_PRELOAD = True
LN_TPOSE = True
P97 = "full" 
LN_N = 16
LN_OPS = 9
QB_LIMIT = 4
DETECT_RACES = False
MASKV = -100000.0
EPS = 1e-5
FMS = 2064



def _hi_lo(a):
    hi = a.astype(np.float32)
    lo = (a - hi.astype(np.float64)).astype(np.float32)
    return hi, lo


def _tile_tt(a):
    w = a.shape[1]
    return np.ascontiguousarray(
        a.reshape(NT, 128, w).transpose(1, 0, 2).reshape(128, NT * w))


def make_tables():
    t = np.arange(T, dtype=np.float64)
    dp = np.arange(0, 64, 2, dtype=np.float64)
    theta = 10000.0 ** (-dp / 64.0)
    fr = t[:, None] * theta[None, :]
    rope = np.concatenate([np.cos(fr), np.sin(fr)], -1)
    c, s = rope[:, 0::2], rope[:, 1::2]
    hi = np.arange(0, 64, 2, dtype=np.float64)
    invf = 1.0 / 10000.0 ** (hi / 64.0)
    fr2 = t[:, None] * invf[None, :]
    fr2 = np.concatenate([fr2, fr2], -1)
    base = (hi + 0.4 * 64) / (1.4 * 64)
    power = (t - T // 2) / 256.0
    scale = base[None, :] ** power[:, None]
    scale = np.concatenate([scale, scale], -1)
    cos, sin = np.cos(fr2), np.sin(fr2)
    out = {}
    for n, a in [("rc", c), ("rs", s), ("xA", cos * scale), ("xB", sin * scale),
                 ("xAi", cos / scale), ("xBi", sin / scale)]:
        h_, l_ = _hi_lo(a)
        out[n + "_hi"] = _tile_tt(h_)
        out[n + "_lo"] = _tile_tt(l_)
    jj = np.arange(128)[:, None]
    ii = np.arange(128)[None, :]
    diag = np.where(ii >= jj, 0.0, MASKV).astype(np.float32)
    anti = np.where(ii <= jj, 0.0, MASKV).astype(np.float32)
    out["mdiag"] = np.ascontiguousarray(np.tile(diag, (1, 4)))
    out["manti"] = np.ascontiguousarray(np.tile(anti, (1, 4)))
    out["idn"] = np.eye(128, dtype=np.float32)
    return out


TABLES = make_tables()



class _Ins:
    __slots__ = ("c", "real")

    def __init__(self, c, real):
        self.c, self.real = c, real

    def then_inc(self, sem, n=1):
        self.c.vals[sem] = self.c.vals.get(sem, 0) + n
        if self.real is not None:
            self.real.then_inc(self.c.semobj[sem], n)
        return self


class _Eng:
    def __init__(self, c, name):
        self.__dict__["c"] = c
        self.__dict__["name"] = name

    def wait(self, sem, val):
        if val > 0 and self.c.active == self.name:
            self.c.eng[self.name].wait_ge(self.c.semobj[sem], val)

    def __getattr__(self, op):
        c, name = self.c, self.name

        def f(*a, **k):
            if c.active == name:
                return _Ins(c, getattr(c.eng[name], op)(*a, **k))
            return _Ins(c, None)
        return f


class Walk:
    def __init__(self, nc, es, semnames):
        self.nc = nc
        self.semobj = {s: es.enter_context(nc.semaphore(s)) for s in semnames}
        self.eng = {"sync": nc.sync, "vector": nc.vector, "scalar": nc.scalar,
                    "tensor": nc.tensor, "gpsimd": nc.gpsimd}
        self.y = _Eng(self, "sync")
        self.v = _Eng(self, "vector")
        self.s = _Eng(self, "scalar")
        self.t = _Eng(self, "tensor")
        self.g = _Eng(self, "gpsimd")
        self.active = None
        self.vals = {}

    def start(self, name):
        self.active = name
        self.vals = {}

    def val(self, sem):
        return self.vals.get(sem, 0)




def build_program(dbg=False):
    nc = bass.Bass("TRN2", target_bir_lowering=False,
                   detect_race_conditions=DETECT_RACES)

    def din(name, shape):
        return nc.dram_tensor(name, shape, F32, kind="ExternalInput").ap()

    x_in = din("x", [T, D])
    WA = {n: din(n, [D, D]) for n in ("Wq", "Wk", "Wv", "Wo")}
    BA = {n: din(n, [1, D]) for n in ("bq", "bk", "bv", "bo")}
    wqkv = [din(f"wqkv{l}", [D, 3 * INNER]) for l in range(2)]
    qsc = [din(f"qsc{l}", [1, 64]) for l in range(2)]
    ksc = [din(f"ksc{l}", [1, 64]) for l in range(2)]
    wol = [din(f"wol{l}", [INNER, D]) for l in range(2)]
    lng = [din(f"lng{l}", [1, D]) for l in range(2)]
    lnb = [din(f"lnb{l}", [1, D]) for l in range(2)]
    w1 = [din(f"w1_{l}", [D, 2 * HID]) for l in range(2)]
    b1 = [din(f"b1_{l}", [1, 2 * HID]) for l in range(2)]
    w2 = [din(f"w2_{l}", [HID, D]) for l in range(2)]
    b2 = [din(f"b2_{l}", [1, D]) for l in range(2)]
    tabs = {k: din("tab_" + k, list(v.shape)) for k, v in TABLES.items()}

    out = nc.dram_tensor("out", [T, D], F32, kind="ExternalOutput").ap()
    xs = nc.dram_tensor("xs", [T, D], F32, kind="Internal").ap()
    xfs = nc.dram_tensor("xfs", [T, D], F32, kind="Internal").ap()

    dbg_t = {}
    if dbg:
        for name, shape, dt in [
                ("d_xn", [T, D], BF16), ("d_q", [T, D], BF16),
                ("d_k", [T, D], BF16), ("d_v", [T, 1040 * NT // NT], BF16),
                ("d_ott", [T, D], BF16), ("d_s1", [T, D], F32),
                ("d_fq", [T, INNER], BF16), ("d_fk", [T, INNER], BF16),
                ("d_fv", [T, 520], BF16), ("d_fott", [T, INNER], BF16),
                ("d_xf1", [T, D], F32), ("d_h2", [T, D], BF16),
                ("d_o2fm", [128, NC * 512], BF16),
                ("d_wot", [128, NC * 1024], BF16),
                ("d_xf2", [T, D], F32)]:
            dbg_t[name] = nc.dram_tensor(name, shape, dt,
                                         kind="ExternalOutput").ap()

    from contextlib import ExitStack
    es = ExitStack()
    with es:
        def sb(name, shape, dt, st=None):
            return (st or es).enter_context(nc.sbuf_tensor(name, shape, dt))

        ttmp = sb("ttmp", [128, 2048], F32)
        btmp = sb("btmp", [128, 1024], BF16)
        btmp2 = sb("btmp2", [128, 1024], BF16)
        stat = sb("stat", [128, 12], F32)
        mv = sb("mv", [128, 2 * NT], F32)
        rst = sb("rst", [128, NT], F32)
        sm1 = sb("sm1", [128, 64], F32)
        cst = sb("cst", [128, 8], F32)
        bvec = sb("bvec", [128, 128], F32)
        bfm = sb("bfm", [128, 64], F32)
        hrep = sb("hrep", [128, 128], F32)
        e1 = sb("e1", [128, 128], BF16)
        e65 = sb("e65", [128, 128], F32)
        rb = sb("rb", [128, 512], F32)
        rsb = sb("rsb", [128, 512], F32)
        idf = sb("idf", [128, 128], F32)
        rows_bf = sb("rows_bf", [128, 4 * 1024], BF16)
        t_rc = sb("t_rc", [128, NT * 32], F32)
        t_rs = sb("t_rs", [128, NT * 32], F32)
        t_rc8 = sb("t_rc8", [128, NT * 32], F32)
        t_rs8 = sb("t_rs8", [128, NT * 32], F32)
        t_xA = sb("t_xA", [128, NT * 64], F32)
        t_xB = sb("t_xB", [128, NT * 64], F32)
        t_xAi = sb("t_xAi", [128, NT * 64], F32)
        t_xBi = sb("t_xBi", [128, NT * 64], F32)
        t_md = sb("t_md", [128, 512], F32)
        t_ma = sb("t_ma", [128, 512], F32)

        psum = es.enter_context(nc.psum_tensor("ps", [128, 4096], F32))

        stA = ExitStack()
        qfm = sb("qfm", [128, NC * 2048], BF16, stA)
        kfm = sb("kfm", [128, NC * 2048], BF16, stA)
        vtt = sb("vtt", [128, NT * 1040], BF16, stA)
        stA1 = ExitStack()
        fmA = sb("fmA", [128, NC * 2048], BF16, stA1)
        wstgF = sb("wstgF", [128, 1024], F32, stA1)
        wstg = sb("wstg", [128, 1024], BF16, stA1)
        wtA = sb("wtA", [128, NC * 1024], BF16, stA1)
        stA1.close()
        stA2 = ExitStack()
        ptA = sb("ptA", [128, 16 * 512], BF16, stA2)
        WoT = sb("WoT", [128, NC * 1024], BF16, stA2)
        wstgF2 = sb("wstgF2", [128, 1024], F32, stA2)
        wstg2 = sb("wstg2", [128, 1024], BF16, stA2)
        oTf = sb("oTf", [128, 512], BF16, stA2)
        o_tt = sb("o_tt", [128, 4 * 1024], BF16, stA2)
        o2fm = sb("o2fm", [128, NC * 512], BF16, stA2)
        stA2.close()
        stA.close()

        stB = ExitStack()
        stBa = ExitStack()
        vttL = sb("vttL", [128, NT * 520], BF16, stBa)
        ptB = sb("ptB", [128, 2 * NC * 512], BF16, stBa)
        oTfB = sb("oTfB", [128, 512], BF16, stBa)
        fmB = sb("fmB", [128, NC * FMS], BF16, stBa)
        wkv = sb("wkv", [128, NC * 1536], BF16, stBa)
        qfmL = sb("qfmL", [128, 4 * 2048], BF16, stBa)
        kfmL = sb("kfmL", [128, 4 * 2048], BF16, stBa)
        o3tt = sb("o3tt", [128, 4 * 512], BF16, stBa)
        o3fm = sb("o3fm", [128, 4 * 512], BF16, stBa)
        woS = sb("woS", [128, 4 * 1024], BF16, stBa)
        tscr = sb("tscr", [128, 3072], F32, stBa)
        stBa.close()
        stBm = ExitStack()
        wgat = sb("wgat", [128, W1C * 1024], BF16, stBm)
        w2s = sb("w2s", [128, W1C * 1024], BF16, stBm)
        xn2 = sb("xn2", [128, NC * 1024], BF16, stBm)
        htt = sb("htt", [128, NT * 1024 // 2], BF16, stBm)
        w1f = sb("w1f", [128, 2 * 2048], F32, stBm)
        w1s = sb("w1s", [128, 2 * 2048], BF16, stBm)
        gel = sb("gel", [128, 1024], BF16, stBm)
        ttm2 = sb("ttm2", [128, 1024], F32, stBm)
        stBm.close()
        stB.close()

        c = Walk(nc, es, ["ld", "lg", "lt", "so",
                          "cv", "cs", "ct", "cg"])

        def phases(c):
            v, s, t_, g, y = c.v, c.s, c.t, c.g, c.y

            def dmaT(src_ap, dst_ap):
                s.dma_start(out=dst_ap, in_=src_ap,
                            transpose=True).then_inc("lt", 16)

            def preload_set(setname):
                if c.active != "scalar":
                    return
                from concourse.hw_specs import get_activation_tables
                eng = c.eng["scalar"]
                for idx, nm in enumerate(get_activation_tables(nc.m.arch)):
                    if nm == setname:
                        eng.add_instruction(mybir.InstLoadActFuncSet(
                            name=nc.get_next_instruction_name(),
                            act_func_set_id=idx, ins=[], outs=[]))
                        return
                raise ValueError(setname)

            def finale():
                for sem in ("ld", "lg", "lt", "so", "cv", "cs", "ct", "cg"):
                    y.wait(sem, c.val(sem))

            def rsqrt_cols(src_ap, dst_ap, floor_col):
                v.tensor_scalar_max(dst_ap, src_ap, cst[:, floor_col:floor_col + 1]
                                    ).then_inc("cv", 1)
                s.wait("cv", c.val("cv"))
                s.activation(dst_ap, dst_ap, ACTF.Ln).then_inc("cs", 1)
                s.wait("cs", c.val("cs"))
                s.activation(dst_ap, dst_ap, ACTF.Exp,
                             scale=-0.5).then_inc("cs", 1)

            v.memset(cst[:, 0:1], -B0)
            v.memset(cst[:, 1:2], EPS)
            v.memset(cst[:, 2:3], 1e-24)
            v.memset(cst[:, 3:4], 1.0)
            v.memset(e1[:, :], 0.0)
            v.memset(e1[0:1, :], 1.0)
            v.memset(e65[:, :], 0.0)
            v.memset(e65[64:65, :], 1.0)
            v.memset(rows_bf[:, :], 0.0)
            v.memset(vtt[:, :], 1.0).then_inc("cv", 1)
            y.dma_start(out=idf[:], in_=tabs["idn"][:, :]).then_inc("ld", 16)

            for n, dst in [("rc", t_rc), ("rs", t_rs), ("xA", t_xA),
                           ("xB", t_xB), ("xAi", t_xAi), ("xBi", t_xBi)]:
                y.dma_start(out=dst[:], in_=tabs[n + "_hi"][:, :]).then_inc("ld", 16)
            y.dma_start(out=t_md[:], in_=tabs["mdiag"][:, :]).then_inc("ld", 16)
            y.dma_start(out=t_ma[:], in_=tabs["manti"][:, :]).then_inc("ld", 16)
            for n, dst in [("rc", t_rc), ("rs", t_rs), ("xA", t_xA),
                           ("xB", t_xB), ("xAi", t_xAi), ("xBi", t_xBi)]:
                w = dst.shape[1] // NT
                y.wait("cv", c.val("cv"))
                y.dma_start(out=ttmp[:, 0:NT * w],
                            in_=tabs[n + "_lo"][:, :]).then_inc("ld", 16)
                v.wait("ld", c.val("ld"))
                v.tensor_tensor(dst[:], dst[:], ttmp[:, 0:NT * w],
                                ALU.add).then_inc("cv", 1)
            v.wait("cv", c.val("cv"))
            v.tensor_scalar_mul(t_rc8[:], t_rc[:], 0.125).then_inc("cv", 1)
            v.wait("cv", c.val("cv"))
            v.tensor_scalar_mul(t_rs8[:], t_rs[:], 0.125).then_inc("cv", 1)

            for i, n in enumerate(("bq", "bk", "bv", "bo")):
                y.wait("cv", c.val("cv"))
                y.dma_start(out=ttmp[0:1, 0:1024], in_=BA[n][:, :]).then_inc("ld", 16)
                v.wait("ld", c.val("ld"))
                v.tensor_scalar_mul(rows_bf[0:1, i * 1024:(i + 1) * 1024],
                                    ttmp[0:1, 0:1024],
                                    -1.0 if n == "bo" else 1.0).then_inc("cv", 1)
            setup_cv = c.val("cv")
            if STAGE < 1:
                finale()
                return
            if STAGE == 90:
                y.dma_start(out=ttmp[:, 0:1024], in_=x_in[0:128, :]).then_inc("ld", 16)
                finale()
                return
            if STAGE in (92, 93, 94, 95):
                y.dma_start(out=ttmp[:, 0:1024], in_=x_in[0:128, :]).then_inc("ld", 16)
                v.wait("ld", c.val("ld"))
                v.bn_stats(stat[:, 0:6], ttmp[:, 0:512])
                v.bn_stats(stat[:, 6:12], ttmp[:, 512:1024]).then_inc("cv", 1)
                v.wait("cv", c.val("cv"))
                v.bn_aggr(mv[:, 0:2], stat[:, 0:12]).then_inc("cv", 1)
                v.wait("cv", c.val("cv"))
                v.tensor_scalar_add(rst[:, 0:1], mv[:, 1:2], cst[:, 1:2]).then_inc("cv", 1)
                s.wait("cv", c.val("cv"))
                s.activation(rst[:, 0:1], rst[:, 0:1], ACTF.Ln).then_inc("cs", 1)
                s.wait("cs", c.val("cs"))
                s.activation(rst[:, 0:1], rst[:, 0:1], ACTF.Exp, scale=-0.5).then_inc("cs", 1)
                if STAGE >= 93:
                    v.wait("cs", c.val("cs"))
                    v.tensor_scalar(btmp[:], ttmp[:, 0:1024], mv[:, 0:1], rst[:, 0:1],
                                    ALU.subtract, ALU.mult).then_inc("cv", 1)
                if STAGE >= 94 and dbg:
                    y.wait("cv", c.val("cv"))
                    y.dma_start(out=dbg_t["d_xn"][0:128, :], in_=btmp[:]).then_inc("so", 16)
                if STAGE >= 95:
                    s.wait("cv", c.val("cv"))
                    for cc in range(NC):
                        dmaT(btmp[:, 128 * cc:128 * (cc + 1)],
                             fmA[:, 2048 * cc:2048 * cc + 128])
                finale()
                return
            if STAGE == 91:
                y.dma_start(out=ttmp[:, 0:1024], in_=x_in[0:128, :]).then_inc("ld", 16)
                v.wait("ld", c.val("ld"))
                v.bn_stats(stat[:, 0:6], ttmp[:, 0:512])
                v.bn_stats(stat[:, 6:12], ttmp[:, 512:1024]).then_inc("cv", 1)
                v.bn_aggr(mv[:, 0:2], stat[:, 0:12]).then_inc("cv", 1)
                finale()
                return

            if USE_PRELOAD:
                preload_set("natural_log_exp_and_others")
            apply_cv = [0] * NT
            for i in range(LN_N):
                sl = (i % 2) * 1024
                y.wait("cv", apply_cv[i - 2] if i >= 2 else setup_cv)
                y.dma_start(out=ttmp[:, sl:sl + 1024],
                            in_=x_in[128 * i:128 * (i + 1), :]).then_inc("ld", 16)
                v.wait("ld", c.val("ld"))
                if LN_OPS >= 1:
                    v.bn_stats(stat[:, 0:6], ttmp[:, sl:sl + 512])
                    v.bn_stats(stat[:, 6:12], ttmp[:, sl + 512:sl + 1024]).then_inc("cv", 1)
                    v.wait("cv", c.val("cv"))
                if LN_OPS >= 2:
                    v.bn_aggr(mv[:, 2 * i:2 * i + 2], stat[:, 0:12]).then_inc("cv", 1)
                    v.wait("cv", c.val("cv"))
                if LN_OPS >= 3:
                    v.tensor_scalar_add(rst[:, i:i + 1], mv[:, 2 * i + 1:2 * i + 2],
                                        cst[:, 1:2]).then_inc("cv", 1)
                if LN_OPS >= 4:
                    s.wait("cv", c.val("cv"))
                    s.activation(rst[:, i:i + 1], rst[:, i:i + 1], ACTF.Ln)
                    s.activation(rst[:, i:i + 1], rst[:, i:i + 1], ACTF.Exp,
                                 scale=-0.5).then_inc("cs", 1)
                v.wait("cs", c.val("cs"))
                v.wait("lt", c.val("lt"))
                if dbg:
                    v.wait("so", c.val("so"))
                if LN_OPS >= 5:
                    v.tensor_scalar(btmp[:], ttmp[:, sl:sl + 1024],
                                    mv[:, 2 * i:2 * i + 1], rst[:, i:i + 1],
                                    ALU.subtract, ALU.mult).then_inc("cv", 1)
                apply_cv[i] = c.val("cv")
                s.wait("cv", c.val("cv"))
                if dbg:
                    y.wait("cv", c.val("cv"))
                    y.dma_start(out=dbg_t["d_xn"][128 * i:128 * (i + 1), :],
                                in_=btmp[:]).then_inc("so", 16)
                if LN_TPOSE:
                    for cc in range(NC):
                        dmaT(btmp[:, 128 * cc:128 * (cc + 1)],
                             fmA[:, 2048 * cc + 128 * i:2048 * cc + 128 * (i + 1)])
            lnA_lt = c.val("lt")
            lnA_cv = c.val("cv")
            if STAGE < 2:
                finale()
                return

            def stage_WT(wsrc, wtdst, stF, stB, nch):
                s.wait("ct", c.val("ct"))
                for q4 in range(8 // nch):
                    y.wait("cv", c.val("cv"))
                    y.dma_start(
                        out=stF[:].rearrange("p (fc k) -> p fc k", fc=nch),
                        in_=wsrc[:, :].rearrange("(fc p) k -> p fc k", p=128)[
                            :, nch * q4:nch * (q4 + 1), :]).then_inc("ld", 16)
                    v.wait("ld", c.val("ld"))
                    v.wait("lt", c.val("lt"))
                    v.tensor_copy(stB[:, 0:1024 * nch],
                                  stF[:, 0:1024 * nch]).then_inc("cv", 1)
                    s.wait("cv", c.val("cv"))
                    for fc2 in range(nch):
                        fc = nch * q4 + fc2
                        for kc in range(NC):
                            dmaT(stB[:, 1024 * fc2 + 128 * kc:
                                     1024 * fc2 + 128 * (kc + 1)],
                                 wtdst[:, 1024 * kc + 128 * fc:
                                       1024 * kc + 128 * (fc + 1)])
                return c.val("lt")

            def proj_tile(i, bias_slot, pb):
                off = 1024 * pb
                for kc in range(NC):
                    lhsT = fmA[:, 2048 * kc + 128 * i:2048 * kc + 128 * (i + 1)]
                    for b_ in range(2):
                        t_.matmul(psum[:, off + 512 * b_:off + 512 * (b_ + 1)],
                                  lhsT, wtA[:, 1024 * kc + 512 * b_:
                                            1024 * kc + 512 * (b_ + 1)],
                                  start=(kc == 0), stop=False)
                for b_ in range(2):
                    ins = t_.matmul(psum[:, off + 512 * b_:off + 512 * (b_ + 1)],
                                    e1[:, :],
                                    rows_bf[:, 1024 * bias_slot + 512 * b_:
                                            1024 * bias_slot + 512 * (b_ + 1)],
                                    start=False, stop=True)
                ins.then_inc("ct", 1)

            def rope_tile(i, c_t, s_t, dst_get, pb):
                off = 1024 * pb
                ps3a = psum[:, off:off + 1024].rearrange(
                    "p (h j two) -> p h (j two)", h=16, two=2)
                x1 = ps3a[:, :, 0::2]
                x2 = ps3a[:, :, 1::2]
                ct3 = c_t[:, 32 * i:32 * (i + 1)].unsqueeze(1).broadcast_to(
                    [128, 16, 32])
                st3 = s_t[:, 32 * i:32 * (i + 1)].unsqueeze(1).broadcast_to(
                    [128, 16, 32])
                m1 = ttmp[:, 0:512].rearrange("p (h j) -> p h j", h=16)
                m2 = ttmp[:, 512:1024].rearrange("p (h j) -> p h j", h=16)
                m3 = ttmp[:, 1024:1536].rearrange("p (h j) -> p h j", h=16)
                m4 = ttmp[:, 1536:2048].rearrange("p (h j) -> p h j", h=16)
                v.tensor_tensor(m1, x1, ct3, ALU.mult)
                v.tensor_tensor(m2, x2, st3, ALU.mult)
                v.tensor_tensor(m3, x1, st3, ALU.mult)
                v.tensor_tensor(m4, x2, ct3, ALU.mult).then_inc("cv", 1)
                v.wait("cv", c.val("cv"))
                v.tensor_tensor(dst_get(0), m1, m2, ALU.subtract)
                v.tensor_tensor(dst_get(1), m3, m4, ALU.add).then_inc("cv", 1)

            if STAGE in (96, 97, 98):
                lt0 = stage_WT(WA["Wk"], wtA, wstgF, wstg, 1)
                if STAGE >= 97:
                    t_.wait("lt", lt0)
                    if P97 == "full":
                        proj_tile(0, 1, 0)
                    elif P97 == "one":
                        t_.matmul(psum[:, 0:512], fmA[:, 0:128], wtA[:, 0:512],
                                  start=True, stop=True)
                        t_.matmul(psum[:, 512:1024], fmA[:, 0:128], wtA[:, 512:1024],
                                  start=True, stop=True).then_inc("ct", 1)
                    elif P97 == "acc8":
                        ins = None
                        for kc in range(NC):
                            for b_ in range(2):
                                ins = t_.matmul(psum[:, 512 * b_:512 * (b_ + 1)],
                                          fmA[:, 2048 * kc:2048 * kc + 128],
                                          wtA[:, 1024 * kc + 512 * b_:1024 * kc + 512 * (b_ + 1)],
                                          start=(kc == 0), stop=(kc == NC - 1))
                        ins.then_inc("ct", 1)
                    v.wait("ct", c.val("ct"))
                    if STAGE >= 98:
                        def dstb98(half):
                            return btmp[:].rearrange("p (h e) -> p h e", h=16)[
                                :, :, 32 * half:32 * (half + 1)]
                        rope_tile(0, t_rc, t_rs, dstb98, 0)
                    else:
                        v.tensor_copy(btmp[:], psum[:, 0:1024]).then_inc("cv", 1)
                finale()
                return
            def rope_from_sbuf(i, src_ap, c_t, s_t, dst_get):
                sr3 = src_ap.rearrange("p (h j two) -> p h (j two)", h=16, two=2)
                x1 = sr3[:, :, 0::2]
                x2 = sr3[:, :, 1::2]
                ct3 = c_t[:, 32 * i:32 * (i + 1)].unsqueeze(1).broadcast_to(
                    [128, 16, 32])
                st3 = s_t[:, 32 * i:32 * (i + 1)].unsqueeze(1).broadcast_to(
                    [128, 16, 32])
                m1 = ttmp[:, 0:512].rearrange("p (h j) -> p h j", h=16)
                m2 = ttmp[:, 512:1024].rearrange("p (h j) -> p h j", h=16)
                m3 = ttmp[:, 1024:1536].rearrange("p (h j) -> p h j", h=16)
                m4 = ttmp[:, 1536:2048].rearrange("p (h j) -> p h j", h=16)
                v.tensor_tensor(m1, x1, ct3, ALU.mult)
                v.tensor_tensor(m2, x2, st3, ALU.mult)
                v.tensor_tensor(m3, x1, st3, ALU.mult)
                v.tensor_tensor(m4, x2, ct3, ALU.mult).then_inc("cv", 1)
                v.wait("cv", c.val("cv"))
                v.tensor_tensor(dst_get(0), m1, m2, ALU.subtract)
                v.tensor_tensor(dst_get(1), m3, m4, ALU.add).then_inc("cv", 1)

            t_.wait("cv", setup_cv)
            for proj in ("k", "v", "q"):
                wsrc = WA["W" + proj]
                bias_slot = {"q": 0, "k": 1, "v": 2}[proj]
                lt0 = stage_WT(wsrc, wtA, wstgF, wstg, 1)
                t_.wait("lt", lt0)
                rope_cv = [0] * NT
                tpose_lt = [0] * NT
                for i in range(NT):
                    pb = i % 2
                    t_.wait("cv", rope_cv[i - 2] if i >= 2 else 0)
                    proj_tile(i, bias_slot, pb)
                    v.wait("ct", c.val("ct"))
                    if proj == "v":
                        def dstv(half, i=i):
                            return vtt[:, 1040 * i:1040 * i + 1040].rearrange(
                                "p (h e) -> p h e", h=16)[:, :, 32 * half:32 * (half + 1)]
                        rope_tile(i, t_rc, t_rs, dstv, pb)
                        rope_cv[i] = c.val("cv")
                        if dbg:
                            y.wait("cv", c.val("cv"))
                            y.dma_start(out=dbg_t["d_v"][128 * i:128 * (i + 1), :],
                                        in_=vtt[:, 1040 * i:1040 * (i + 1)]
                                        ).then_inc("so", 16)
                    else:
                        ctab = t_rc8 if proj == "q" else t_rc
                        stab = t_rs8 if proj == "q" else t_rs

                        def dstb(half):
                            return btmp[:].rearrange("p (h e) -> p h e", h=16)[
                                :, :, 32 * half:32 * (half + 1)]
                        v.wait("lt", tpose_lt[i - 1] if i >= 1 else 0)
                        if dbg:
                            v.wait("so", c.val("so"))
                        rope_tile(i, ctab, stab, dstb, pb)
                        rope_cv[i] = c.val("cv")
                        dstfm = qfm if proj == "q" else kfm
                        s.wait("cv", c.val("cv"))
                        if dbg:
                            y.wait("cv", c.val("cv"))
                            y.dma_start(out=dbg_t["d_" + proj][128 * i:128 * (i + 1), :],
                                        in_=btmp[:]).then_inc("so", 16)
                        for rc_ in range(NC):
                            dmaT(btmp[:, 128 * rc_:128 * (rc_ + 1)],
                                 dstfm[:, 2048 * rc_ + 128 * i:
                                       2048 * rc_ + 128 * (i + 1)])
                        tpose_lt[i] = c.val("lt")
            projA_cv = c.val("cv")
            projA_lt = c.val("lt")
            if STAGE < 3:
                finale()
                return

            v.memset(rb[:, :], 0.0).then_inc("cv", 1)
            wo_lt = stage_WT(WA["Wo"], WoT, wstgF2, wstg2, 1)
            if dbg:
                y.wait("lt", wo_lt)
                y.dma_start(out=dbg_t["d_wot"][:, :], in_=WoT[:, :]).then_inc("so", 16)
            groups = [list(range(3 * g_, min(3 * g_ + 3, 16))) for g_ in range(6)]
            exp_cs = {0: 0, 1: 0}
            pv_clear_cs = 0
            pt_free_ct = 0
            pv_clear_cv = 0
            o_tt_lt = 0
            oTf_lt = {0: 0, 1: 0}
            wo_ct = 0
            rope_cv = 0
            for qb in range(QB_LIMIT):
                for h in range(16):
                    kbase = 64 * (h % 2)
                    cbase = 2048 * (h // 2)
                    for gi, grp in enumerate(groups):
                        par = gi % 2
                        bank0 = 1 if par == 0 else 4
                        t_.wait("cs", exp_cs[par])
                        if gi == 0:
                            t_.wait("lt", wo_lt if (qb == 0 and h == 0) else 0)
                            t_.wait("ct", pt_free_ct)
                            if h == 0:
                                t_.wait("cv", c.val("cv"))
                        ins = None
                        for j, ki in enumerate(grp):
                            ins = t_.matmul(
                                psum[:, 512 * (bank0 + j):512 * (bank0 + j + 1)],
                                kfm[kbase:kbase + 64,
                                    cbase + 128 * ki:cbase + 128 * (ki + 1)],
                                qfm[kbase:kbase + 64,
                                    cbase + 512 * qb:cbase + 512 * (qb + 1)],
                                start=True, stop=True)
                        ins.then_inc("ct", 1)
                        s.wait("ct", c.val("ct"))
                        s.activation(ptA[:, 512 * grp[0]:512 * (grp[-1] + 1)],
                                     psum[:, 512 * bank0:
                                          512 * (bank0 + len(grp))],
                                     ACTF.Exp, bias=cst[:, 0:1]).then_inc("cs", 1)
                        exp_cs[par] = c.val("cs")
                        t_.wait("cs", c.val("cs"))
                        if gi == 0:
                            t_.wait("cv", pv_clear_cv)
                        ins = None
                        for ki in grp:
                            ins = t_.matmul(
                                psum[0:65, 0:512],
                                vtt[:, 1040 * ki + 65 * h:1040 * ki + 65 * (h + 1)],
                                ptA[:, 512 * ki:512 * (ki + 1)],
                                start=(ki == 0), stop=(ki == 15))
                        if grp[-1] == 15:
                            ins.then_inc("ct", 1)
                    pt_free_ct = c.val("ct")
                    s.wait("ct", c.val("ct"))
                    s.activation(rb[64:65, 0:512], psum[64:65, 0:512],
                                 ACTF.Ln).then_inc("cs", 1)
                    s.wait("cs", c.val("cs"))
                    s.activation(rb[64:65, 0:512], rb[64:65, 0:512], ACTF.Exp,
                                 scale=-1.0).then_inc("cs", 1)
                    t_.wait("cs", c.val("cs"))
                    t_.matmul(psum[:, 3584:4096], e65[:, :], rb[:, 0:512],
                              start=True, stop=True).then_inc("ct", 1)
                    v.wait("ct", c.val("ct"))
                    v.tensor_copy(rsb[:, 0:512], psum[:, 3584:4096]).then_inc("cv", 1)
                    v.wait("cv", c.val("cv"))
                    v.wait("lt", oTf_lt[h % 2])
                    ob = 64 * (h % 2)
                    v.tensor_tensor(oTf[ob:ob + 64, 0:512], psum[0:64, 0:512],
                                    rsb[0:64, 0:512], ALU.mult).then_inc("cv", 1)
                    pv_clear_cs = 0
                    pv_clear_cv = c.val("cv")
                    s.wait("cv", c.val("cv"))
                    if h == 0:
                        s.wait("cv", rope_cv)
                    for qs in range(4):
                        dmaT(oTf[ob:ob + 64, 128 * qs:128 * (qs + 1)],
                             o_tt[:, 1024 * qs + 64 * h:1024 * qs + 64 * (h + 1)])
                    oTf_lt[h % 2] = c.val("lt")
                    o_tt_lt = c.val("lt")
                v.wait("lt", o_tt_lt)
                onorm_cv = c.val("cv")
                if dbg:
                    y.wait("lt", o_tt_lt)
                    for qs in range(4):
                        y.dma_start(out=dbg_t["d_ott"][128 * (4 * qb + qs):
                                                       128 * (4 * qb + qs + 1), :],
                                    in_=o_tt[:, 1024 * qs:1024 * (qs + 1)]
                                    ).then_inc("so", 16)
                for qs in range(4):
                    i = 4 * qb + qs
                    v.wait("lt", o_tt_lt)
                    v.wait("lt", c.val("lt"))
                    if dbg:
                        v.wait("so", c.val("so"))

                    def dsto(half, qs=qs):
                        return btmp[:].rearrange("p (h e) -> p h e", h=16)[
                            :, :, 32 * half:32 * (half + 1)]
                    rope_from_sbuf(i, o_tt[:, 1024 * qs:1024 * (qs + 1)],
                                   t_rc, t_rs, dsto)
                    s.wait("cv", c.val("cv"))
                    if dbg:
                        y.wait("cv", c.val("cv"))
                        y.dma_start(out=dbg_t["d_h2"][128 * i:128 * (i + 1), :],
                                    in_=btmp[:]).then_inc("so", 16)
                        s.wait("so", c.val("so"))
                    if qs == 0:
                        s.wait("ct", wo_ct)
                    for rc_ in range(NC):
                        dmaT(btmp[:, 128 * rc_:128 * (rc_ + 1)],
                             o2fm[:, 512 * rc_ + 128 * qs:512 * rc_ + 128 * (qs + 1)])
                if dbg and qb == 0:
                    y.wait("lt", c.val("lt"))
                    y.dma_start(out=dbg_t["d_o2fm"][:, :],
                                in_=o2fm[:, :]).then_inc("so", 16)
                t_.wait("lt", c.val("lt"))
                if dbg and qb == 0:
                    t_.wait("so", c.val("so"))
                for qs in range(4):
                    i = 4 * qb + qs
                    t_.wait("cv", c.val("cv"))
                    ins = None
                    for rc_ in range(NC):
                        for b_ in range(2):
                            ins = t_.matmul(
                                psum[:, 512 * (1 + b_):512 * (2 + b_)],
                                o2fm[:, 512 * rc_ + 128 * qs:512 * rc_ + 128 * (qs + 1)],
                                WoT[:, 1024 * rc_ + 512 * b_:1024 * rc_ + 512 * (b_ + 1)],
                                start=(rc_ == 0), stop=False)
                    for b_ in range(2):
                        ins = t_.matmul(psum[:, 512 * (1 + b_):512 * (2 + b_)],
                                        e1[:, :],
                                        rows_bf[:, 3 * 1024 + 512 * b_:
                                                3 * 1024 + 512 * (b_ + 1)],
                                        start=False, stop=True)
                    ins.then_inc("ct", 1)
                    if dbg:
                        v.wait("ct", c.val("ct"))
                        v.tensor_copy(ttmp[:, 1024:2048],
                                      psum[:, 512:1536]).then_inc("cv", 1)
                        y.wait("cv", c.val("cv"))
                        y.dma_start(out=dbg_t["d_xf1"][128 * i:128 * (i + 1), :],
                                    in_=ttmp[:, 1024:2048]).then_inc("so", 16)
                        v.wait("so", c.val("so"))
                    y.wait("cv", c.val("cv"))
                    y.dma_start(out=ttmp[:, 0:1024],
                                in_=x_in[128 * i:128 * (i + 1), :]).then_inc("ld", 16)
                    v.wait("ld", c.val("ld"))
                    v.wait("ct", c.val("ct"))
                    v.tensor_sub(ttmp[:, 1024:2048], ttmp[:, 0:1024],
                                 psum[:, 512:1536]).then_inc("cv", 1)
                    y.wait("cv", c.val("cv"))
                    y.dma_start(out=xs[128 * i:128 * (i + 1), :],
                                in_=ttmp[:, 1024:2048]).then_inc("ld", 16)
                    if dbg:
                        y.dma_start(out=dbg_t["d_s1"][128 * i:128 * (i + 1), :],
                                    in_=ttmp[:, 1024:2048]).then_inc("so", 16)
                    y.wait("ld", c.val("ld"))
                    if dbg:
                        y.wait("so", c.val("so"))
                wo_ct = c.val("ct")
                rope_cv = c.val("cv")
            if STAGE < 4:
                finale()
                return
            phase_B(c)
            finale()
            return

        def phase_B(c):
            v, s, t_, g, y = c.v, c.s, c.t, c.g, c.y

            def dmaT(src_ap, dst_ap):
                s.dma_start(out=dst_ap, in_=src_ap,
                            transpose=True).then_inc("lt", 16)

            def bar():
                for e_ in (y, v, s, t_, g):
                    for sem in ("ld", "lg", "lt", "so", "cv", "cs", "ct", "cg"):
                        e_.wait(sem, c.val(sem))

            def preload_set(setname):
                if c.active != "scalar":
                    return
                from concourse.hw_specs import get_activation_tables
                eng = c.eng["scalar"]
                for idx, nm in enumerate(get_activation_tables(nc.m.arch)):
                    if nm == setname:
                        eng.add_instruction(mybir.InstLoadActFuncSet(
                            name=nc.get_next_instruction_name(),
                            act_func_set_id=idx, ins=[], outs=[]))
                        return
                raise ValueError(setname)

            bar()
            v.memset(rb[:, :], 0.0).then_inc("cv", 1)
            apply_cv = [0] * NT
            for i in range(NT):
                sl = (i % 2) * 1024
                y.wait("cv", apply_cv[i - 2] if i >= 2 else 0)
                y.wait("ld", c.val("ld"))
                y.dma_start(out=ttmp[:, sl:sl + 1024],
                            in_=xs[128 * i:128 * (i + 1), :]).then_inc("ld", 16)
                v.wait("ld", c.val("ld"))
                v.bn_stats(stat[:, 0:6], ttmp[:, sl:sl + 512])
                v.bn_stats(stat[:, 6:12], ttmp[:, sl + 512:sl + 1024]).then_inc("cv", 1)
                v.wait("cv", c.val("cv"))
                v.bn_aggr(mv[:, 2 * i:2 * i + 2], stat[:, 0:12]).then_inc("cv", 1)
                v.wait("cv", c.val("cv"))
                v.tensor_scalar_add(rst[:, i:i + 1], mv[:, 2 * i + 1:2 * i + 2],
                                    cst[:, 1:2]).then_inc("cv", 1)
                s.wait("cv", c.val("cv"))
                s.activation(rst[:, i:i + 1], rst[:, i:i + 1],
                             ACTF.Ln).then_inc("cs", 1)
                s.wait("cs", c.val("cs"))
                s.activation(rst[:, i:i + 1], rst[:, i:i + 1], ACTF.Exp,
                             scale=-0.5).then_inc("cs", 1)
                v.wait("cs", c.val("cs"))
                v.tensor_scalar(ttm2[:], ttmp[:, sl:sl + 1024],
                                mv[:, 2 * i:2 * i + 1], rst[:, i:i + 1],
                                ALU.subtract, ALU.mult).then_inc("cv", 1)
                apply_cv[i] = c.val("cv")
                y.wait("cv", c.val("cv"))
                y.dma_start(out=xfs[128 * i:128 * (i + 1), :],
                            in_=ttm2[:]).then_inc("ld", 16)
                if dbg:
                    y.dma_start(out=dbg_t["d_xf1"][128 * i:128 * (i + 1), :],
                                in_=ttm2[:]).then_inc("so", 16)
                v.wait("lt", c.val("lt"))
                v.tensor_copy(btmp[:], ttm2[:]).then_inc("cv", 1)
                apply_cv[i] = c.val("cv")
                s.wait("cv", c.val("cv"))
                for cc in range(NC):
                    dmaT(btmp[:, 128 * cc:128 * (cc + 1)],
                         fmB[:, FMS * cc + 16 + 128 * i:FMS * cc + 16 + 128 * (i + 1)])
            v.wait("lt", c.val("lt"))
            z = None
            for cc in range(NC):
                z = v.memset(fmB[:, FMS * cc + 15:FMS * cc + 16], 0.0)
            z.then_inc("cv", 1)
            fmB_cv = c.val("cv")
            fmB_lt = c.val("lt")

            for l in range(2):
                bar()
                v.memset(ptB[:, :], 0.0)
                v.memset(vttL[:, :], 1.0).then_inc("cv", 1)
                if l == 1:
                    z = None
                    for cc in range(NC):
                        z = v.memset(fmB[:, FMS * cc:FMS * cc + 16], 0.0)
                    z.then_inc("cv", 1)
                    reb_cv = [0] * NT
                    for i in range(NT):
                        sl = (i % 2) * 1024
                        y.wait("cv", reb_cv[i - 2] if i >= 2 else c.val("cv"))
                        y.wait("ld", c.val("ld"))
                        y.dma_start(out=ttmp[:, sl:sl + 1024],
                                    in_=xfs[128 * i:128 * (i + 1), :]).then_inc("ld", 16)
                        v.wait("ld", c.val("ld"))
                        v.wait("lt", c.val("lt"))
                        v.tensor_copy(btmp[:], ttmp[:, sl:sl + 1024]).then_inc("cv", 1)
                        reb_cv[i] = c.val("cv")
                        s.wait("cv", c.val("cv"))
                        for cc in range(NC):
                            dmaT(btmp[:, 128 * cc:128 * (cc + 1)],
                                 fmB[:, FMS * cc + 16 + 128 * i:
                                     FMS * cc + 16 + 128 * (i + 1)])
                    fmB_cv = c.val("cv")
                    fmB_lt = c.val("lt")

                for q4 in range(4):
                    y.wait("cv", c.val("cv"))
                    y.dma_start(
                        out=tscr[:, 0:3072].rearrange("p (kc f) -> p kc f", kc=2),
                        in_=wqkv[l][:, :].rearrange("(kc p) f -> p kc f", p=128)[
                            :, 2 * q4:2 * q4 + 2, :]).then_inc("ld", 16)
                    v.wait("ld", c.val("ld"))
                    ins = None
                    for kc2 in range(2):
                        kc = 2 * q4 + kc2
                        ins = v.tensor_copy(wkv[:, 1536 * kc:1536 * (kc + 1)],
                                            tscr[:, 1536 * kc2:1536 * (kc2 + 1)])
                    ins.then_inc("cv", 1)
                wkv_cv = c.val("cv")
                for q2 in range(2):
                    y.wait("cv", c.val("cv"))
                    y.dma_start(
                        out=tscr[:, 0:2048].rearrange("p (kc f) -> p kc f", kc=2),
                        in_=wol[l][:, :].rearrange("(kc p) f -> p kc f", p=128)[
                            :, 2 * q2:2 * q2 + 2, :]).then_inc("ld", 16)
                    v.wait("ld", c.val("ld"))
                    v.tensor_copy(woS[:, 2048 * q2:2048 * (q2 + 1)],
                                  tscr[:, 0:2048]).then_inc("cv", 1)
                for idx, (srcrow, scl) in enumerate([(qsc[l], 8.0), (ksc[l], 1.0)]):
                    y.wait("cv", c.val("cv"))
                    y.dma_start(out=tscr[64:65, 0:64], in_=srcrow[:, :]).then_inc("ld", 16)
                    v.wait("ld", c.val("ld"))
                    v.memset(rb[:, 0:64], 0.0).then_inc("cv", 1)
                    v.wait("cv", c.val("cv"))
                    v.tensor_copy(rb[64:65, 0:64], tscr[64:65, 0:64]).then_inc("cv", 1)
                    t_.wait("cv", c.val("cv"))
                    t_.matmul(psum[:, 3072:3136], e65[:, :], rb[:, 0:64],
                              start=True, stop=True).then_inc("ct", 1)
                    v.wait("ct", c.val("ct"))
                    v.tensor_scalar_mul(hrep[:, 64 * idx:64 * (idx + 1)],
                                        psum[:, 3072:3136], scl).then_inc("cv", 1)
                hrep_cv = c.val("cv")
                if STAGE < 5:
                    continue

                t_.wait("cv", max(wkv_cv, fmB_cv, hrep_cv))
                t_.wait("lt", fmB_lt)
                prep_cv = [0] * NT
                tp_lt = [0] * NT
                for i in range(NT):
                    pb = i % 2
                    po = 1536 * pb
                    t_.wait("cv", prep_cv[i - 2] if i >= 2 else 0)
                    ins = None
                    for kc in range(NC):
                        off = FMS * kc + (16 if kc < 4 else 15) + 128 * i
                        lhsT = fmB[:, off:off + 128]
                        for b_ in range(3):
                            ins = t_.matmul(
                                psum[:, po + 512 * b_:po + 512 * (b_ + 1)],
                                lhsT, wkv[:, 1536 * kc + 512 * b_:
                                          1536 * kc + 512 * (b_ + 1)],
                                start=(kc == 0), stop=(kc == NC - 1))
                    ins.then_inc("ct", 1)
                    s.wait("ct", c.val("ct"))
                    s.copy(tscr[:, 0:512], psum[:, po:po + 512])
                    s.copy(tscr[:, 512:1024], psum[:, po + 512:po + 1024]).then_inc("cs", 1)
                    v.wait("ct", c.val("ct"))
                    v3 = vttL[:, 520 * i:520 * (i + 1)].rearrange(
                        "p (h e) -> p h e", h=8)[:, :, 0:64]
                    v.tensor_copy(v3, psum[:, po + 1024:po + 1536].rearrange(
                        "p (h d) -> p h d", h=8)).then_inc("cv", 1)
                    prep_cv[i] = c.val("cv")
                    v.wait("cs", c.val("cs"))
                    v.tensor_tensor(tscr[:, 1024:1536], tscr[:, 0:512],
                                    tscr[:, 0:512], ALU.mult)
                    v.tensor_tensor(tscr[:, 1536:2048], tscr[:, 512:1024],
                                    tscr[:, 512:1024], ALU.mult).then_inc("cv", 1)
                    v.wait("cv", c.val("cv"))
                    v.tensor_reduce(sm1[:, 0:8].rearrange("p (h o) -> p h o", h=8),
                                    tscr[:, 1024:1536].rearrange("p (h d) -> p h d", h=8),
                                    axis=AX.X, op=ALU.add)
                    v.tensor_reduce(sm1[:, 8:16].rearrange("p (h o) -> p h o", h=8),
                                    tscr[:, 1536:2048].rearrange("p (h d) -> p h d", h=8),
                                    axis=AX.X, op=ALU.add).then_inc("cv", 1)
                    v.wait("cv", c.val("cv"))
                    v.tensor_scalar_max(sm1[:, 16:32], sm1[:, 0:16],
                                        cst[:, 2:3]).then_inc("cv", 1)
                    s.wait("cv", c.val("cv"))
                    s.activation(sm1[:, 16:32], sm1[:, 16:32],
                                 ACTF.Ln).then_inc("cs", 1)
                    s.wait("cs", c.val("cs"))
                    s.activation(sm1[:, 16:32], sm1[:, 16:32], ACTF.Exp,
                                 scale=-0.5).then_inc("cs", 1)
                    v.wait("cs", c.val("cs"))
                    ins = None
                    for hh in range(8):
                        ins = v.tensor_scalar_mul(
                            tscr[:, 64 * hh:64 * (hh + 1)],
                            tscr[:, 64 * hh:64 * (hh + 1)],
                            sm1[:, 16 + hh:17 + hh])
                        ins = v.tensor_scalar_mul(
                            tscr[:, 512 + 64 * hh:512 + 64 * (hh + 1)],
                            tscr[:, 512 + 64 * hh:512 + 64 * (hh + 1)],
                            sm1[:, 24 + hh:25 + hh])
                    ins.then_inc("cv", 1)
                    v.wait("cv", c.val("cv"))
                    q3 = tscr[:, 0:512].rearrange("p (h d) -> p h d", h=8)
                    k3 = tscr[:, 512:1024].rearrange("p (h d) -> p h d", h=8)
                    v.tensor_tensor(q3, q3, hrep[:, 0:64].unsqueeze(1).broadcast_to(
                        [128, 8, 64]), ALU.mult)
                    v.tensor_tensor(k3, k3, hrep[:, 64:128].unsqueeze(1).broadcast_to(
                        [128, 8, 64]), ALU.mult).then_inc("cv", 1)
                    v.wait("cv", c.val("cv"))
                    v.wait("lt", tp_lt[i - 1] if i >= 1 else 0)
                    for (src3, tA, tB, dstb) in ((q3, t_xA, t_xB, btmp),
                                                 (k3, t_xAi, t_xBi, btmp2)):
                        a3 = tA[:, 64 * i:64 * (i + 1)].unsqueeze(1).broadcast_to(
                            [128, 8, 64])
                        b3 = tB[:, 64 * i:64 * (i + 1)].unsqueeze(1).broadcast_to(
                            [128, 8, 64])
                        m1 = ttmp[:, 0:512].rearrange("p (h d) -> p h d", h=8)
                        m2 = ttmp[:, 512:1024].rearrange("p (h d) -> p h d", h=8)
                        v.tensor_tensor(m1, src3, a3, ALU.mult)
                        v.tensor_tensor(m2, src3, b3, ALU.mult).then_inc("cv", 1)
                        v.wait("cv", c.val("cv"))
                        d3 = dstb[:, 0:512].rearrange("p (h d) -> p h d", h=8)
                        v.tensor_tensor(d3[:, :, 0:32], m1[:, :, 0:32],
                                        m2[:, :, 32:64], ALU.subtract)
                        v.tensor_tensor(d3[:, :, 32:64], m1[:, :, 32:64],
                                        m2[:, :, 0:32], ALU.add).then_inc("cv", 1)
                    prep_cv[i] = c.val("cv")
                    if dbg and l == 0:
                        y.wait("cv", c.val("cv"))
                        y.dma_start(out=dbg_t["d_fq"][128 * i:128 * (i + 1), :],
                                    in_=btmp[:, 0:512]).then_inc("so", 16)
                        y.dma_start(out=dbg_t["d_fk"][128 * i:128 * (i + 1), :],
                                    in_=btmp2[:, 0:512]).then_inc("so", 16)
                        y.dma_start(out=dbg_t["d_fv"][128 * i:128 * (i + 1), :],
                                    in_=vttL[:, 520 * i:520 * (i + 1)]).then_inc("so", 16)
                        s.wait("so", c.val("so"))
                    s.wait("cv", c.val("cv"))
                    for rc_ in range(4):
                        dmaT(btmp[:, 128 * rc_:128 * (rc_ + 1)],
                             qfmL[:, 2048 * rc_ + 128 * i:2048 * rc_ + 128 * (i + 1)])
                        dmaT(btmp2[:, 128 * rc_:128 * (rc_ + 1)],
                             kfmL[:, 2048 * rc_ + 128 * i:2048 * rc_ + 128 * (i + 1)])
                    tp_lt[i] = c.val("lt")
                prepB_lt = c.val("lt")
                if STAGE < 6:
                    continue

                t_.wait("lt", prepB_lt)
                exp_done_cs = 0
                ptb_free_ct = {0: 0, 1: 0}
                oTfB_lt = {0: 0, 1: 0}
                pvB_clear_cv = 0
                o3tt_lt = 0
                o3_rd_cv = 0
                woB_ct = 0
                uidx = 0
                for w in range(4):
                    for h in range(HL):
                        par = uidx % 2
                        pto = 4096 * par
                        kb = 64 * (h % 2)
                        cb = 2048 * (h // 2)
                        first_c = 4 if w == 0 else 0
                        for half in range(2):
                            cr = [cc for cc in range(4 * half, 4 * half + 4)
                                  if cc >= first_c]
                            if not cr:
                                continue
                            t_.wait("cs", exp_done_cs)
                            if half == 0 or w == 0:
                                t_.wait("ct", ptb_free_ct[par])
                            ins = None
                            for cc in cr:
                                vt = (w - 1) * 4 + cc
                                alo, ahi = max(0, cc - 4), min(3, cc)
                                ins = t_.matmul(
                                    psum[:, 512 * (cc % 4) + 128 * alo:
                                         512 * (cc % 4) + 128 * (ahi + 1)],
                                    kfmL[kb:kb + 64, cb + 128 * vt:cb + 128 * (vt + 1)],
                                    qfmL[kb:kb + 64, cb + 512 * w + 128 * alo:
                                         cb + 512 * w + 128 * (ahi + 1)],
                                    start=True, stop=True)
                            ins.then_inc("ct", 1)
                            v.wait("ct", c.val("ct"))
                            blk = psum[:, 0:2560].rearrange(
                                "p (b e) -> p b e", b=4)[:, :, 0:128]
                            mt = (t_md if half == 1 else t_ma)[:].rearrange(
                                "p (b e) -> p b e", b=4)
                            v.tensor_tensor(blk, blk, mt, ALU.add).then_inc("cv", 1)
                            s.wait("cv", c.val("cv"))
                            ins = None
                            for cc in cr:
                                alo, ahi = max(0, cc - 4), min(3, cc)
                                ins = s.activation(
                                    ptB[:, pto + 512 * cc + 128 * alo:
                                        pto + 512 * cc + 128 * (ahi + 1)],
                                    psum[:, 512 * (cc % 4) + 128 * alo:
                                         512 * (cc % 4) + 128 * (ahi + 1)],
                                    ACTF.Exp, bias=cst[:, 0:1])
                            ins.then_inc("cs", 1)
                            exp_done_cs = c.val("cs")
                            t_.wait("cs", exp_done_cs)
                            if half == 0 or w == 0:
                                t_.wait("cv", pvB_clear_cv)
                            ins = None
                            for cc in cr:
                                vt = (w - 1) * 4 + cc
                                ins = t_.matmul(
                                    psum[0:65, 3584:4096],
                                    vttL[:, 520 * vt + 65 * h:520 * vt + 65 * (h + 1)],
                                    ptB[:, pto + 512 * cc:pto + 512 * (cc + 1)],
                                    start=(cc == first_c), stop=(cc == 7))
                            if cr[-1] == 7:
                                ins.then_inc("ct", 1)
                        ptb_free_ct[par] = c.val("ct")
                        s.wait("ct", c.val("ct"))
                        s.activation(rb[64:65, 0:512], psum[64:65, 3584:4096],
                                     ACTF.Ln).then_inc("cs", 1)
                        s.wait("cs", c.val("cs"))
                        s.activation(rb[64:65, 0:512], rb[64:65, 0:512], ACTF.Exp,
                                     scale=-1.0).then_inc("cs", 1)
                        t_.wait("cs", c.val("cs"))
                        t_.matmul(psum[:, 3072:3584], e65[:, :], rb[:, 0:512],
                                  start=True, stop=True).then_inc("ct", 1)
                        v.wait("ct", c.val("ct"))
                        v.tensor_copy(rsb[:, 0:512], psum[:, 3072:3584]).then_inc("cv", 1)
                        v.wait("cv", c.val("cv"))
                        v.wait("lt", oTfB_lt[h % 2])
                        ob = 64 * (h % 2)
                        v.tensor_tensor(oTfB[ob:ob + 64, 0:512],
                                        psum[0:64, 3584:4096],
                                        rsb[0:64, 0:512], ALU.mult).then_inc("cv", 1)
                        pvB_clear_cv = c.val("cv")
                        s.wait("cv", c.val("cv"))
                        if h == 0:
                            s.wait("cv", o3_rd_cv)
                        for sub in range(4):
                            dmaT(oTfB[ob:ob + 64, 128 * sub:128 * (sub + 1)],
                                 o3tt[:, 512 * sub + 64 * h:512 * sub + 64 * (h + 1)])
                        oTfB_lt[h % 2] = c.val("lt")
                        o3tt_lt = c.val("lt")
                        uidx += 1
                    if dbg and l == 0:
                        y.wait("lt", o3tt_lt)
                        for sub in range(4):
                            y.dma_start(
                                out=dbg_t["d_fott"][128 * (4 * w + sub):
                                                    128 * (4 * w + sub + 1), :],
                                in_=o3tt[:, 512 * sub:512 * (sub + 1)]).then_inc("so", 16)
                        s.wait("so", c.val("so"))
                    s.wait("lt", o3tt_lt)
                    s.wait("ct", woB_ct)
                    for sub in range(4):
                        for rc_ in range(4):
                            dmaT(o3tt[:, 512 * sub + 128 * rc_:
                                      512 * sub + 128 * (rc_ + 1)],
                                 o3fm[:, 512 * rc_ + 128 * sub:
                                      512 * rc_ + 128 * (sub + 1)])
                    o3fm_lt = c.val("lt")
                    t_.wait("lt", o3fm_lt)
                    for sub in range(4):
                        i = 4 * w + sub
                        t_.wait("cv", c.val("cv"))
                        ins = None
                        for rc_ in range(4):
                            for b_ in range(2):
                                ins = t_.matmul(
                                    psum[:, 2048 + 512 * b_:2048 + 512 * (b_ + 1)],
                                    o3fm[:, 512 * rc_ + 128 * sub:
                                         512 * rc_ + 128 * (sub + 1)],
                                    woS[:, 1024 * rc_ + 512 * b_:
                                        1024 * rc_ + 512 * (b_ + 1)],
                                    start=(rc_ == 0), stop=(rc_ == 3))
                        ins.then_inc("ct", 1)
                        y.wait("cv", c.val("cv"))
                        y.wait("ld", c.val("ld"))
                        y.dma_start(out=ttmp[:, 0:1024],
                                    in_=xfs[128 * i:128 * (i + 1), :]).then_inc("ld", 16)
                        v.wait("ld", c.val("ld"))
                        v.wait("ct", c.val("ct"))
                        v.tensor_add(ttmp[:, 1024:2048], ttmp[:, 0:1024],
                                     psum[:, 2048:3072]).then_inc("cv", 1)
                        y.wait("cv", c.val("cv"))
                        y.dma_start(out=xfs[128 * i:128 * (i + 1), :],
                                    in_=ttmp[:, 1024:2048]).then_inc("ld", 16)
                        y.wait("ld", c.val("ld"))
                    woB_ct = c.val("ct")
                    o3_rd_cv = c.val("cv")
                if STAGE < 7:
                    continue

                bar()
                for kc in range(W1C):
                    rows = 128 if kc < W1C - 1 else LAST
                    y.wait("cv", c.val("cv"))
                    y.dma_start(out=w1f[0:rows, 0:1024],
                                in_=w2[l][128 * kc:128 * kc + rows, :]).then_inc("ld", 16)
                    v.wait("ld", c.val("ld"))
                    v.tensor_copy(w2s[0:rows, 1024 * kc:1024 * (kc + 1)],
                                  w1f[0:rows, 0:1024]).then_inc("cv", 1)
                y.wait("cv", c.val("cv"))
                y.dma_start(out=w1f[0:1, 0:1024], in_=b2[l][:, :]).then_inc("ld", 16)
                v.wait("ld", c.val("ld"))
                v.tensor_copy(rows_bf[0:1, 0:1024], w1f[0:1, 0:1024]).then_inc("cv", 1)
                slabs = [(lng[l], 0, 8, 0), (lnb[l], 0, 8, 8),
                         (b1[l], 0, 22, 16), (b1[l], HID, 22, 38)]
                for srct, off0, prows, bcol in slabs:
                    y.wait("cv", c.val("cv"))
                    if prows == 8:
                        y.dma_start(out=w1f[0:8, 0:128],
                                    in_=srct[:, :].rearrange("o (c f) -> (o c) f", c=8)
                                    ).then_inc("ld", 16)
                    else:
                        y.dma_start(out=w1f[0:21, 0:128],
                                    in_=srct[0:1, off0:off0 + 2688].rearrange(
                                        "o (c f) -> (o c) f", c=21)).then_inc("ld", 16)
                        y.dma_start(out=w1f[21:22, 0:LAST],
                                    in_=srct[0:1, off0 + 2688:off0 + HID]).then_inc("ld", 16)
                    t_.wait("ld", c.val("ld"))
                    t_.transpose(psum[0:128, 3072:3072 + prows],
                                 w1f[0:prows, 0:128],
                                 idf[0:prows, 0:prows]).then_inc("ct", 1)
                    v.wait("ct", c.val("ct"))
                    v.tensor_copy(bfm[:, bcol:bcol + prows],
                                  psum[:, 3072:3072 + prows]).then_inc("cv", 1)
                bfm_cv = c.val("cv")

                for hf in range(2):
                    if USE_PRELOAD:
                        preload_set("natural_log_exp_and_others")
                    ln2_cv = [0] * 8
                    for k in range(8):
                        ti = 8 * hf + k
                        sl = (k % 2) * 1024
                        y.wait("cv", ln2_cv[k - 2] if k >= 2 else c.val("cv"))
                        y.wait("ld", c.val("ld"))
                        y.dma_start(out=ttmp[:, sl:sl + 512],
                                    in_=xfs[128 * ti:128 * (ti + 1), 0:512]
                                    ).then_inc("ld", 16)
                        if ti == 0:
                            y.dma_start(out=ttmp[1:128, sl + 512:sl + 1024],
                                        in_=xfs[0:127, 512:1024]).then_inc("ld", 16)
                        else:
                            y.dma_start(out=ttmp[:, sl + 512:sl + 1024],
                                        in_=xfs[128 * ti - 1:128 * (ti + 1) - 1,
                                                512:1024]).then_inc("ld", 16)
                        v.wait("ld", c.val("ld"))
                        if ti == 0:
                            v.memset(ttmp[0:1, sl + 512:sl + 1024], 0.0)
                        v.tensor_copy(htt[:, 1024 * k:1024 * (k + 1)],
                                      ttmp[:, sl:sl + 1024]).then_inc("cv", 1)
                        v.wait("cv", c.val("cv"))
                        v.bn_stats(stat[:, 0:6], htt[:, 1024 * k:1024 * k + 512])
                        v.bn_stats(stat[:, 6:12],
                                   htt[:, 1024 * k + 512:1024 * (k + 1)]).then_inc("cv", 1)
                        v.wait("cv", c.val("cv"))
                        v.bn_aggr(mv[:, 2 * k:2 * k + 2], stat[:, 0:12]).then_inc("cv", 1)
                        v.wait("cv", c.val("cv"))
                        v.tensor_scalar_add(rst[:, k:k + 1], mv[:, 2 * k + 1:2 * k + 2],
                                            cst[:, 1:2]).then_inc("cv", 1)
                        s.wait("cv", c.val("cv"))
                        s.activation(rst[:, k:k + 1], rst[:, k:k + 1],
                                     ACTF.Ln).then_inc("cs", 1)
                        s.wait("cs", c.val("cs"))
                        s.activation(rst[:, k:k + 1], rst[:, k:k + 1], ACTF.Exp,
                                     scale=-0.5).then_inc("cs", 1)
                        v.wait("cs", c.val("cs"))
                        v.wait("lt", c.val("lt"))
                        v.tensor_scalar(btmp[:], htt[:, 1024 * k:1024 * (k + 1)],
                                        mv[:, 2 * k:2 * k + 1], rst[:, k:k + 1],
                                        ALU.subtract, ALU.mult).then_inc("cv", 1)
                        ln2_cv[k] = c.val("cv")
                        if dbg and l == 0:
                            y.wait("cv", c.val("cv"))
                            y.dma_start(out=dbg_t["d_h2"][128 * ti:128 * (ti + 1), :],
                                        in_=btmp[:]).then_inc("so", 16)
                            s.wait("so", c.val("so"))
                        s.wait("cv", c.val("cv"))
                        for cc in range(NC):
                            dmaT(btmp[:, 128 * cc:128 * (cc + 1)],
                                 xn2[:, 1024 * cc + 128 * k:1024 * cc + 128 * (k + 1)])
                    v.wait("lt", c.val("lt"))
                    v.wait("cv", bfm_cv)
                    ins = None
                    for cc in range(NC):
                        ins = v.tensor_scalar(xn2[:, 1024 * cc:1024 * (cc + 1)],
                                              xn2[:, 1024 * cc:1024 * (cc + 1)],
                                              bfm[:, cc:cc + 1], bfm[:, 8 + cc:9 + cc],
                                              ALU.mult, ALU.add)
                    ins.then_inc("cv", 1)
                    xn2_cv = c.val("cv")
                    if USE_PRELOAD:
                        s.wait("cv", xn2_cv)
                        preload_set("gelu_and_others")
                    t_.wait("cv", xn2_cv)
                    cast_cv = [0, 0]
                    gate_cv = [0, 0]
                    for cc in range(W1C):
                        rows = 128 if cc < W1C - 1 else LAST
                        par = cc % 2
                        wo_ = 2048 * par
                        y.wait("cv", cast_cv[par])
                        y.wait("ld", c.val("ld"))
                        y.dma_start(
                            out=w1f[:, wo_:wo_ + 1024].rearrange(
                                "p (kc f) -> p kc f", kc=8)[:, :, 0:rows],
                            in_=w1[l][:, 128 * cc:128 * cc + rows].rearrange(
                                "(kc p) f -> p kc f", p=128)).then_inc("ld", 16)
                        y.dma_start(
                            out=w1f[:, wo_ + 1024:wo_ + 2048].rearrange(
                                "p (kc f) -> p kc f", kc=8)[:, :, 0:rows],
                            in_=w1[l][:, HID + 128 * cc:HID + 128 * cc + rows].rearrange(
                                "(kc p) f -> p kc f", p=128)).then_inc("ld", 16)
                        v.wait("ld", c.val("ld"))
                        v.tensor_copy(w1s[:, wo_:wo_ + 2048],
                                      w1f[:, wo_:wo_ + 2048]).then_inc("cv", 1)
                        cast_cv[par] = c.val("cv")
                        t_.wait("cv", max(cast_cv[par], gate_cv[par]))
                        ins = None
                        for kc8 in range(NC):
                            for b_ in range(2):
                                ins = t_.matmul(
                                    psum[0:rows, wo_ + 512 * b_:wo_ + 512 * (b_ + 1)],
                                    w1s[:, wo_ + 128 * kc8:wo_ + 128 * kc8 + rows],
                                    xn2[:, 1024 * kc8 + 512 * b_:
                                        1024 * kc8 + 512 * (b_ + 1)],
                                    start=(kc8 == 0), stop=(kc8 == NC - 1))
                                ins = t_.matmul(
                                    psum[0:rows, wo_ + 1024 + 512 * b_:
                                         wo_ + 1024 + 512 * (b_ + 1)],
                                    w1s[:, wo_ + 1024 + 128 * kc8:
                                        wo_ + 1024 + 128 * kc8 + rows],
                                    xn2[:, 1024 * kc8 + 512 * b_:
                                        1024 * kc8 + 512 * (b_ + 1)],
                                    start=(kc8 == 0), stop=(kc8 == NC - 1))
                        ins.then_inc("ct", 1)
                        s.wait("ct", c.val("ct"))
                        s.activation(gel[0:rows, 0:1024],
                                     psum[0:rows, wo_ + 1024:wo_ + 2048],
                                     ACTF.Gelu, bias=bfm[0:rows, 38 + cc:39 + cc]
                                     ).then_inc("cs", 1)
                        v.wait("cs", c.val("cs"))
                        v.scalar_tensor_tensor(
                            wgat[0:rows, 1024 * cc:1024 * (cc + 1)],
                            psum[0:rows, wo_:wo_ + 1024],
                            bfm[0:rows, 16 + cc:17 + cc],
                            gel[0:rows, 0:1024],
                            ALU.add, ALU.mult).then_inc("cv", 1)
                        gate_cv[par] = c.val("cv")
                    gate_all_cv = c.val("cv")
                    t_.wait("cv", gate_all_cv)
                    for k in range(8):
                        ti = 8 * hf + k
                        pb = k % 2
                        wo_ = 2048 * pb
                        t_.wait("cv", c.val("cv"))
                        ins = None
                        for kc in range(W1C):
                            rows = 128 if kc < W1C - 1 else LAST
                            for b_ in range(2):
                                ins = t_.matmul(
                                    psum[:, wo_ + 512 * b_:wo_ + 512 * (b_ + 1)],
                                    wgat[0:rows, 1024 * kc + 128 * k:
                                         1024 * kc + 128 * (k + 1)],
                                    w2s[0:rows, 1024 * kc + 512 * b_:
                                        1024 * kc + 512 * (b_ + 1)],
                                    start=(kc == 0), stop=False)
                        for b_ in range(2):
                            ins = t_.matmul(
                                psum[:, wo_ + 512 * b_:wo_ + 512 * (b_ + 1)],
                                e1[:, :], rows_bf[:, 512 * b_:512 * (b_ + 1)],
                                start=False, stop=True)
                        ins.then_inc("ct", 1)
                        y.wait("cv", c.val("cv"))
                        y.wait("ld", c.val("ld"))
                        y.dma_start(out=ttmp[:, 0:1024],
                                    in_=xfs[128 * ti:128 * (ti + 1), :]).then_inc("ld", 16)
                        if l == 1:
                            y.dma_start(out=ttm2[:, 0:1024],
                                        in_=xs[128 * ti:128 * (ti + 1), :]).then_inc("ld", 16)
                        v.wait("ld", c.val("ld"))
                        v.wait("ct", c.val("ct"))
                        v.tensor_add(ttmp[:, 1024:2048], ttmp[:, 0:1024],
                                     psum[:, wo_:wo_ + 1024]).then_inc("cv", 1)
                        if l == 1:
                            v.wait("cv", c.val("cv"))
                            v.tensor_add(ttmp[:, 1024:2048], ttmp[:, 1024:2048],
                                         ttm2[:, 0:1024]).then_inc("cv", 1)
                        y.wait("cv", c.val("cv"))
                        dstt = out if l == 1 else xfs
                        y.dma_start(out=dstt[128 * ti:128 * (ti + 1), :],
                                    in_=ttmp[:, 1024:2048]).then_inc("ld", 16)
                        if dbg and l == 0:
                            y.dma_start(out=dbg_t["d_xf2"][128 * ti:128 * (ti + 1), :],
                                        in_=ttmp[:, 1024:2048]).then_inc("so", 16)
                        y.wait("ld", c.val("ld"))
                        if dbg:
                            y.wait("so", c.val("so"))
                if USE_PRELOAD:
                    preload_set("natural_log_exp_and_others")
                bar()

        c.start(None)
        phases(c)
        expect = dict(c.vals)
        blk = es.enter_context(nc.Block())
        for en in ["sync", "vector", "scalar", "tensor", "gpsimd"]:
            def run(eng, en=en):
                c.start(en)
                phases(c)
                assert c.vals == expect, (en, c.vals, expect)
            getattr(blk, en)(run)
    return nc, dbg_t


def _core_inputs(xi, tabs):
    d = {"x": np.ascontiguousarray(xi)}
    d.update(tabs)
    return d


def kernel(**inputs):
    x = np.asarray(inputs["L0_attn_x"])
    nc, _ = build_program(dbg=False)
    base = {
        "Wq": inputs["L0_attn_Wq"], "Wk": inputs["L0_attn_Wk"],
        "Wv": inputs["L0_attn_Wv"], "Wo": inputs["L0_attn_Wo"],
        "bq": inputs["L0_attn_bq"].reshape(1, D),
        "bk": inputs["L0_attn_bk"].reshape(1, D),
        "bv": inputs["L0_attn_bv"].reshape(1, D),
        "bo": inputs["L0_attn_bo"].reshape(1, D),
    }
    for l in range(2):
        base[f"wqkv{l}"] = inputs["L0_ffn_wqkv"][l]
        base[f"qsc{l}"] = inputs["L0_ffn_q_scale"][l].reshape(1, 64)
        base[f"ksc{l}"] = inputs["L0_ffn_k_scale"][l].reshape(1, 64)
        base[f"wol{l}"] = inputs["L0_ffn_wo"][l]
        base[f"lng{l}"] = inputs["L0_ffn_ln_g"][l].reshape(1, D)
        base[f"lnb{l}"] = inputs["L0_ffn_ln_b"][l].reshape(1, D)
        base[f"w1_{l}"] = inputs["L0_ffn_w1"][l]
        base[f"b1_{l}"] = inputs["L0_ffn_b1"][l].reshape(1, 2 * HID)
        base[f"w2_{l}"] = inputs["L0_ffn_w2"][l]
        base[f"b2_{l}"] = inputs["L0_ffn_b2"][l].reshape(1, D)
    for k, v_ in TABLES.items():
        base["tab_" + k] = v_
    in_maps = []
    for b in range(8):
        m = dict(base)
        m["x"] = np.ascontiguousarray(x[b])
        in_maps.append(m)
    res = run_bass_kernel_spmd(nc, in_maps, core_ids=list(range(8)))
    return np.stack([r["out"] for r in res.results], 0).astype(np.float32)


# version 06
# speedup vs baseline: 1.0787x; 1.0787x over previous
import math
import numpy as np
import ml_dtypes

import concourse.bass as bass
import concourse.mybir as mybir
from concourse.bass_utils import run_bass_kernel_spmd

F32 = mybir.dt.float32
BF16 = mybir.dt.bfloat16
AX = mybir.AxisListType
ALU = mybir.AluOpType
ACTF = mybir.ActivationFunctionType

T = 2048
D = 1024
NT = 16
NC = 8
HG = 16
HL = 8
INNER = 512
HID = 2730
W1C = 22
LAST = 2730 - 21 * 128
B0 = 40.0
STAGE = 99
USE_PRELOAD = True
LN_TPOSE = True
P97 = "full" 
LN_N = 16
LN_OPS = 9
QB_LIMIT = 4
DETECT_RACES = False
MASKV = -100000.0
EPS = 1e-5
FMS = 2064



def _hi_lo(a):
    hi = a.astype(np.float32)
    lo = (a - hi.astype(np.float64)).astype(np.float32)
    return hi, lo


def _tile_tt(a):
    w = a.shape[1]
    return np.ascontiguousarray(
        a.reshape(NT, 128, w).transpose(1, 0, 2).reshape(128, NT * w))


def make_tables():
    t = np.arange(T, dtype=np.float64)
    dp = np.arange(0, 64, 2, dtype=np.float64)
    theta = 10000.0 ** (-dp / 64.0)
    fr = t[:, None] * theta[None, :]
    rope = np.concatenate([np.cos(fr), np.sin(fr)], -1)
    c, s = rope[:, 0::2], rope[:, 1::2]
    hi = np.arange(0, 64, 2, dtype=np.float64)
    invf = 1.0 / 10000.0 ** (hi / 64.0)
    fr2 = t[:, None] * invf[None, :]
    fr2 = np.concatenate([fr2, fr2], -1)
    base = (hi + 0.4 * 64) / (1.4 * 64)
    power = (t - T // 2) / 256.0
    scale = base[None, :] ** power[:, None]
    scale = np.concatenate([scale, scale], -1)
    cos, sin = np.cos(fr2), np.sin(fr2)
    out = {}
    for n, a in [("rc", c), ("rs", s), ("xA", cos * scale), ("xB", sin * scale),
                 ("xAi", cos / scale), ("xBi", sin / scale)]:
        h_, l_ = _hi_lo(a)
        out[n + "_hi"] = _tile_tt(h_)
        out[n + "_lo"] = _tile_tt(l_)
    jj = np.arange(128)[:, None]
    ii = np.arange(128)[None, :]
    diag = np.where(ii >= jj, 0.0, MASKV).astype(np.float32)
    anti = np.where(ii <= jj, 0.0, MASKV).astype(np.float32)
    out["mdiag"] = np.ascontiguousarray(np.tile(diag, (1, 4)))
    out["manti"] = np.ascontiguousarray(np.tile(anti, (1, 4)))
    out["idn"] = np.eye(128, dtype=np.float32)
    return out


TABLES = make_tables()



class _Ins:
    __slots__ = ("c", "real")

    def __init__(self, c, real):
        self.c, self.real = c, real

    def then_inc(self, sem, n=1):
        self.c.vals[sem] = self.c.vals.get(sem, 0) + n
        if self.real is not None:
            self.real.then_inc(self.c.semobj[sem], n)
        return self


class _Eng:
    def __init__(self, c, name):
        self.__dict__["c"] = c
        self.__dict__["name"] = name

    def wait(self, sem, val):
        if val > 0 and self.c.active == self.name:
            self.c.eng[self.name].wait_ge(self.c.semobj[sem], val)

    def __getattr__(self, op):
        c, name = self.c, self.name

        def f(*a, **k):
            if c.active == name:
                return _Ins(c, getattr(c.eng[name], op)(*a, **k))
            return _Ins(c, None)
        return f


class Walk:
    def __init__(self, nc, es, semnames):
        self.nc = nc
        self.semobj = {s: es.enter_context(nc.semaphore(s)) for s in semnames}
        self.eng = {"sync": nc.sync, "vector": nc.vector, "scalar": nc.scalar,
                    "tensor": nc.tensor, "gpsimd": nc.gpsimd}
        self.y = _Eng(self, "sync")
        self.v = _Eng(self, "vector")
        self.s = _Eng(self, "scalar")
        self.t = _Eng(self, "tensor")
        self.g = _Eng(self, "gpsimd")
        self.active = None
        self.vals = {}

    def start(self, name):
        self.active = name
        self.vals = {}

    def val(self, sem):
        return self.vals.get(sem, 0)




def build_program(dbg=False):
    nc = bass.Bass("TRN2", target_bir_lowering=False,
                   detect_race_conditions=DETECT_RACES)

    def din(name, shape):
        return nc.dram_tensor(name, shape, F32, kind="ExternalInput").ap()

    x_in = din("x", [T, D])
    WA = {n: din(n, [D, D]) for n in ("Wq", "Wk", "Wv", "Wo")}
    BA = {n: din(n, [1, D]) for n in ("bq", "bk", "bv", "bo")}
    wqkv = [din(f"wqkv{l}", [D, 3 * INNER]) for l in range(2)]
    qsc = [din(f"qsc{l}", [1, 64]) for l in range(2)]
    ksc = [din(f"ksc{l}", [1, 64]) for l in range(2)]
    wol = [din(f"wol{l}", [INNER, D]) for l in range(2)]
    lng = [din(f"lng{l}", [1, D]) for l in range(2)]
    lnb = [din(f"lnb{l}", [1, D]) for l in range(2)]
    w1 = [din(f"w1_{l}", [D, 2 * HID]) for l in range(2)]
    b1 = [din(f"b1_{l}", [1, 2 * HID]) for l in range(2)]
    w2 = [din(f"w2_{l}", [HID, D]) for l in range(2)]
    b2 = [din(f"b2_{l}", [1, D]) for l in range(2)]
    tabs = {k: din("tab_" + k, list(v.shape)) for k, v in TABLES.items()}

    out = nc.dram_tensor("out", [T, D], F32, kind="ExternalOutput").ap()
    xs = nc.dram_tensor("xs", [T, D], F32, kind="Internal").ap()
    xfs = nc.dram_tensor("xfs", [T, D], F32, kind="Internal").ap()

    dbg_t = {}
    if dbg:
        for name, shape, dt in [
                ("d_xn", [T, D], BF16), ("d_q", [T, D], BF16),
                ("d_k", [T, D], BF16), ("d_v", [T, 1040 * NT // NT], BF16),
                ("d_ott", [T, D], BF16), ("d_s1", [T, D], F32),
                ("d_fq", [T, INNER], BF16), ("d_fk", [T, INNER], BF16),
                ("d_fv", [T, 520], BF16), ("d_fott", [T, INNER], BF16),
                ("d_xf1", [T, D], F32), ("d_h2", [T, D], BF16),
                ("d_o2fm", [128, NC * 512], BF16),
                ("d_wot", [128, NC * 1024], BF16),
                ("d_xf2", [T, D], F32)]:
            dbg_t[name] = nc.dram_tensor(name, shape, dt,
                                         kind="ExternalOutput").ap()

    from contextlib import ExitStack
    es = ExitStack()
    with es:
        def sb(name, shape, dt, st=None):
            return (st or es).enter_context(nc.sbuf_tensor(name, shape, dt))

        ttmp = sb("ttmp", [128, 2048], F32)
        btmp = sb("btmp", [128, 1024], BF16)
        btmp2 = sb("btmp2", [128, 1024], BF16)
        stat = sb("stat", [128, 12], F32)
        mv = sb("mv", [128, 2 * NT], F32)
        rst = sb("rst", [128, NT], F32)
        sm1 = sb("sm1", [128, 64], F32)
        cst = sb("cst", [128, 8], F32)
        bvec = sb("bvec", [128, 128], F32)
        bfm = sb("bfm", [128, 64], F32)
        hrep = sb("hrep", [128, 128], F32)
        e1 = sb("e1", [128, 128], BF16)
        e65 = sb("e65", [128, 128], F32)
        rb = sb("rb", [128, 512], F32)
        rsb = sb("rsb", [128, 512], F32)
        idf = sb("idf", [128, 128], F32)
        rows_bf = sb("rows_bf", [128, 4 * 1024], BF16)
        t_rc = sb("t_rc", [128, NT * 32], F32)
        t_rs = sb("t_rs", [128, NT * 32], F32)
        t_rc8 = sb("t_rc8", [128, NT * 32], F32)
        t_rs8 = sb("t_rs8", [128, NT * 32], F32)
        t_xA = sb("t_xA", [128, NT * 64], F32)
        t_xB = sb("t_xB", [128, NT * 64], F32)
        t_xAi = sb("t_xAi", [128, NT * 64], F32)
        t_xBi = sb("t_xBi", [128, NT * 64], F32)
        t_md = sb("t_md", [128, 512], F32)
        t_ma = sb("t_ma", [128, 512], F32)

        psum = es.enter_context(nc.psum_tensor("ps", [128, 4096], F32))

        stA = ExitStack()
        qfm = sb("qfm", [128, NC * 2048], BF16, stA)
        kfm = sb("kfm", [128, NC * 2048], BF16, stA)
        vtt = sb("vtt", [128, NT * 1040], BF16, stA)
        stA1 = ExitStack()
        fmA = sb("fmA", [128, NC * 2048], BF16, stA1)
        wstgF = sb("wstgF", [128, 1024], F32, stA1)
        wstg = sb("wstg", [128, 1024], BF16, stA1)
        wtA = sb("wtA", [128, NC * 1024], BF16, stA1)
        stA1.close()
        stA2 = ExitStack()
        ptA = sb("ptA", [128, 16 * 512], BF16, stA2)
        WoT = sb("WoT", [128, NC * 1024], BF16, stA2)
        wstgF2 = sb("wstgF2", [128, 1024], F32, stA2)
        wstg2 = sb("wstg2", [128, 1024], BF16, stA2)
        oTf = sb("oTf", [128, 512], BF16, stA2)
        o_tt = sb("o_tt", [128, 4 * 1024], BF16, stA2)
        o2fm = sb("o2fm", [128, NC * 512], BF16, stA2)
        stA2.close()
        stA.close()

        stB = ExitStack()
        stBa = ExitStack()
        vttL = sb("vttL", [128, NT * 520], BF16, stBa)
        ptB = sb("ptB", [128, 2 * NC * 512], BF16, stBa)
        oTfB = sb("oTfB", [128, 512], BF16, stBa)
        fmB = sb("fmB", [128, NC * FMS], BF16, stBa)
        wkv = sb("wkv", [128, NC * 1536], BF16, stBa)
        qfmL = sb("qfmL", [128, 4 * 2048], BF16, stBa)
        kfmL = sb("kfmL", [128, 4 * 2048], BF16, stBa)
        o3tt = sb("o3tt", [128, 4 * 512], BF16, stBa)
        o3fm = sb("o3fm", [128, 4 * 512], BF16, stBa)
        woS = sb("woS", [128, 4 * 1024], BF16, stBa)
        tscr = sb("tscr", [128, 3072], F32, stBa)
        stBa.close()
        stBm = ExitStack()
        wgat = sb("wgat", [128, W1C * 1024], BF16, stBm)
        w2s = sb("w2s", [128, W1C * 1024], BF16, stBm)
        xn2 = sb("xn2", [128, NC * 1024], BF16, stBm)
        htt = sb("htt", [128, NT * 1024 // 2], BF16, stBm)
        w1f = sb("w1f", [128, 2 * 2048], F32, stBm)
        w1s = sb("w1s", [128, 2 * 2048], BF16, stBm)
        gel = sb("gel", [128, 1024], BF16, stBm)
        ttm2 = sb("ttm2", [128, 1024], F32, stBm)
        stBm.close()
        stB.close()

        c = Walk(nc, es, ["ld", "lg", "lt", "so",
                          "cv", "cs", "ct", "cg"])

        def phases(c):
            v, s, t_, g, y = c.v, c.s, c.t, c.g, c.y

            def dmaT(src_ap, dst_ap):
                s.dma_start(out=dst_ap, in_=src_ap,
                            transpose=True).then_inc("lt", 16)

            def preload_set(setname):
                if c.active != "scalar":
                    return
                from concourse.hw_specs import get_activation_tables
                eng = c.eng["scalar"]
                for idx, nm in enumerate(get_activation_tables(nc.m.arch)):
                    if nm == setname:
                        eng.add_instruction(mybir.InstLoadActFuncSet(
                            name=nc.get_next_instruction_name(),
                            act_func_set_id=idx, ins=[], outs=[]))
                        return
                raise ValueError(setname)

            def finale():
                for sem in ("ld", "lg", "lt", "so", "cv", "cs", "ct", "cg"):
                    y.wait(sem, c.val(sem))

            def rsqrt_cols(src_ap, dst_ap, floor_col):
                v.tensor_scalar_max(dst_ap, src_ap, cst[:, floor_col:floor_col + 1]
                                    ).then_inc("cv", 1)
                s.wait("cv", c.val("cv"))
                s.activation(dst_ap, dst_ap, ACTF.Ln).then_inc("cs", 1)
                s.wait("cs", c.val("cs"))
                s.activation(dst_ap, dst_ap, ACTF.Exp,
                             scale=-0.5).then_inc("cs", 1)

            v.memset(cst[:, 0:1], -B0)
            v.memset(cst[:, 1:2], EPS)
            v.memset(cst[:, 2:3], 1e-24)
            v.memset(cst[:, 3:4], 1.0)
            v.memset(e1[:, :], 0.0)
            v.memset(e1[0:1, :], 1.0)
            v.memset(e65[:, :], 0.0)
            v.memset(e65[64:65, :], 1.0)
            v.memset(rows_bf[:, :], 0.0)
            v.memset(vtt[:, :], 1.0).then_inc("cv", 1)
            y.dma_start(out=idf[:], in_=tabs["idn"][:, :]).then_inc("ld", 16)

            for n, dst in [("rc", t_rc), ("rs", t_rs), ("xA", t_xA),
                           ("xB", t_xB), ("xAi", t_xAi), ("xBi", t_xBi)]:
                y.dma_start(out=dst[:], in_=tabs[n + "_hi"][:, :]).then_inc("ld", 16)
            y.dma_start(out=t_md[:], in_=tabs["mdiag"][:, :]).then_inc("ld", 16)
            y.dma_start(out=t_ma[:], in_=tabs["manti"][:, :]).then_inc("ld", 16)
            for n, dst in [("rc", t_rc), ("rs", t_rs), ("xA", t_xA),
                           ("xB", t_xB), ("xAi", t_xAi), ("xBi", t_xBi)]:
                w = dst.shape[1] // NT
                y.wait("cv", c.val("cv"))
                y.dma_start(out=ttmp[:, 0:NT * w],
                            in_=tabs[n + "_lo"][:, :]).then_inc("ld", 16)
                v.wait("ld", c.val("ld"))
                v.tensor_tensor(dst[:], dst[:], ttmp[:, 0:NT * w],
                                ALU.add).then_inc("cv", 1)
            v.wait("cv", c.val("cv"))
            v.tensor_scalar_mul(t_rc8[:], t_rc[:], 0.125).then_inc("cv", 1)
            v.wait("cv", c.val("cv"))
            v.tensor_scalar_mul(t_rs8[:], t_rs[:], 0.125).then_inc("cv", 1)

            for i, n in enumerate(("bq", "bk", "bv", "bo")):
                y.wait("cv", c.val("cv"))
                y.dma_start(out=ttmp[0:1, 0:1024], in_=BA[n][:, :]).then_inc("ld", 16)
                v.wait("ld", c.val("ld"))
                v.tensor_scalar_mul(rows_bf[0:1, i * 1024:(i + 1) * 1024],
                                    ttmp[0:1, 0:1024],
                                    -1.0 if n == "bo" else 1.0).then_inc("cv", 1)
            setup_cv = c.val("cv")
            if STAGE < 1:
                finale()
                return
            if STAGE == 90:
                y.dma_start(out=ttmp[:, 0:1024], in_=x_in[0:128, :]).then_inc("ld", 16)
                finale()
                return
            if STAGE in (92, 93, 94, 95):
                y.dma_start(out=ttmp[:, 0:1024], in_=x_in[0:128, :]).then_inc("ld", 16)
                v.wait("ld", c.val("ld"))
                v.bn_stats(stat[:, 0:6], ttmp[:, 0:512])
                v.bn_stats(stat[:, 6:12], ttmp[:, 512:1024]).then_inc("cv", 1)
                v.wait("cv", c.val("cv"))
                v.bn_aggr(mv[:, 0:2], stat[:, 0:12]).then_inc("cv", 1)
                v.wait("cv", c.val("cv"))
                v.tensor_scalar_add(rst[:, 0:1], mv[:, 1:2], cst[:, 1:2]).then_inc("cv", 1)
                s.wait("cv", c.val("cv"))
                s.activation(rst[:, 0:1], rst[:, 0:1], ACTF.Ln).then_inc("cs", 1)
                s.wait("cs", c.val("cs"))
                s.activation(rst[:, 0:1], rst[:, 0:1], ACTF.Exp, scale=-0.5).then_inc("cs", 1)
                if STAGE >= 93:
                    v.wait("cs", c.val("cs"))
                    v.tensor_scalar(btmp[:], ttmp[:, 0:1024], mv[:, 0:1], rst[:, 0:1],
                                    ALU.subtract, ALU.mult).then_inc("cv", 1)
                if STAGE >= 94 and dbg:
                    y.wait("cv", c.val("cv"))
                    y.dma_start(out=dbg_t["d_xn"][0:128, :], in_=btmp[:]).then_inc("so", 16)
                if STAGE >= 95:
                    s.wait("cv", c.val("cv"))
                    for cc in range(NC):
                        dmaT(btmp[:, 128 * cc:128 * (cc + 1)],
                             fmA[:, 2048 * cc:2048 * cc + 128])
                finale()
                return
            if STAGE == 91:
                y.dma_start(out=ttmp[:, 0:1024], in_=x_in[0:128, :]).then_inc("ld", 16)
                v.wait("ld", c.val("ld"))
                v.bn_stats(stat[:, 0:6], ttmp[:, 0:512])
                v.bn_stats(stat[:, 6:12], ttmp[:, 512:1024]).then_inc("cv", 1)
                v.bn_aggr(mv[:, 0:2], stat[:, 0:12]).then_inc("cv", 1)
                finale()
                return

            if USE_PRELOAD:
                preload_set("natural_log_exp_and_others")
            apply_cv = [0] * NT
            for i in range(LN_N):
                sl = (i % 2) * 1024
                y.wait("cv", apply_cv[i - 2] if i >= 2 else setup_cv)
                y.dma_start(out=ttmp[:, sl:sl + 1024],
                            in_=x_in[128 * i:128 * (i + 1), :]).then_inc("ld", 16)
                v.wait("ld", c.val("ld"))
                if LN_OPS >= 1:
                    v.bn_stats(stat[:, 0:6], ttmp[:, sl:sl + 512])
                    v.bn_stats(stat[:, 6:12], ttmp[:, sl + 512:sl + 1024]).then_inc("cv", 1)
                    v.wait("cv", c.val("cv"))
                if LN_OPS >= 2:
                    v.bn_aggr(mv[:, 2 * i:2 * i + 2], stat[:, 0:12]).then_inc("cv", 1)
                    v.wait("cv", c.val("cv"))
                if LN_OPS >= 3:
                    v.tensor_scalar_add(rst[:, i:i + 1], mv[:, 2 * i + 1:2 * i + 2],
                                        cst[:, 1:2]).then_inc("cv", 1)
                if LN_OPS >= 4:
                    s.wait("cv", c.val("cv"))
                    s.activation(rst[:, i:i + 1], rst[:, i:i + 1], ACTF.Ln)
                    s.activation(rst[:, i:i + 1], rst[:, i:i + 1], ACTF.Exp,
                                 scale=-0.5).then_inc("cs", 1)
                v.wait("cs", c.val("cs"))
                v.wait("lt", c.val("lt"))
                if dbg:
                    v.wait("so", c.val("so"))
                if LN_OPS >= 5:
                    v.tensor_scalar(btmp[:], ttmp[:, sl:sl + 1024],
                                    mv[:, 2 * i:2 * i + 1], rst[:, i:i + 1],
                                    ALU.subtract, ALU.mult).then_inc("cv", 1)
                apply_cv[i] = c.val("cv")
                s.wait("cv", c.val("cv"))
                if dbg:
                    y.wait("cv", c.val("cv"))
                    y.dma_start(out=dbg_t["d_xn"][128 * i:128 * (i + 1), :],
                                in_=btmp[:]).then_inc("so", 16)
                if LN_TPOSE:
                    for cc in range(NC):
                        dmaT(btmp[:, 128 * cc:128 * (cc + 1)],
                             fmA[:, 2048 * cc + 128 * i:2048 * cc + 128 * (i + 1)])
            lnA_lt = c.val("lt")
            lnA_cv = c.val("cv")
            if STAGE < 2:
                finale()
                return

            def stage_WT(wsrc, wtdst, stF, stB, nch):
                s.wait("ct", c.val("ct"))
                for q4 in range(8 // nch):
                    y.wait("cv", c.val("cv"))
                    y.dma_start(
                        out=stF[:].rearrange("p (fc k) -> p fc k", fc=nch),
                        in_=wsrc[:, :].rearrange("(fc p) k -> p fc k", p=128)[
                            :, nch * q4:nch * (q4 + 1), :]).then_inc("ld", 16)
                    v.wait("ld", c.val("ld"))
                    v.wait("lt", c.val("lt"))
                    v.tensor_copy(stB[:, 0:1024 * nch],
                                  stF[:, 0:1024 * nch]).then_inc("cv", 1)
                    s.wait("cv", c.val("cv"))
                    for fc2 in range(nch):
                        fc = nch * q4 + fc2
                        for kc in range(NC):
                            dmaT(stB[:, 1024 * fc2 + 128 * kc:
                                     1024 * fc2 + 128 * (kc + 1)],
                                 wtdst[:, 1024 * kc + 128 * fc:
                                       1024 * kc + 128 * (fc + 1)])
                return c.val("lt")

            def proj_tile(i, bias_slot, pb):
                off = 1024 * pb
                for kc in range(NC):
                    lhsT = fmA[:, 2048 * kc + 128 * i:2048 * kc + 128 * (i + 1)]
                    for b_ in range(2):
                        t_.matmul(psum[:, off + 512 * b_:off + 512 * (b_ + 1)],
                                  lhsT, wtA[:, 1024 * kc + 512 * b_:
                                            1024 * kc + 512 * (b_ + 1)],
                                  start=(kc == 0), stop=False)
                for b_ in range(2):
                    ins = t_.matmul(psum[:, off + 512 * b_:off + 512 * (b_ + 1)],
                                    e1[:, :],
                                    rows_bf[:, 1024 * bias_slot + 512 * b_:
                                            1024 * bias_slot + 512 * (b_ + 1)],
                                    start=False, stop=True)
                ins.then_inc("ct", 1)

            def rope_tile(i, c_t, s_t, dst_get, pb):
                off = 1024 * pb
                ps3a = psum[:, off:off + 1024].rearrange(
                    "p (h j two) -> p h (j two)", h=16, two=2)
                x1 = ps3a[:, :, 0::2]
                x2 = ps3a[:, :, 1::2]
                ct3 = c_t[:, 32 * i:32 * (i + 1)].unsqueeze(1).broadcast_to(
                    [128, 16, 32])
                st3 = s_t[:, 32 * i:32 * (i + 1)].unsqueeze(1).broadcast_to(
                    [128, 16, 32])
                m1 = ttmp[:, 0:512].rearrange("p (h j) -> p h j", h=16)
                m2 = ttmp[:, 512:1024].rearrange("p (h j) -> p h j", h=16)
                m3 = ttmp[:, 1024:1536].rearrange("p (h j) -> p h j", h=16)
                m4 = ttmp[:, 1536:2048].rearrange("p (h j) -> p h j", h=16)
                v.tensor_tensor(m1, x1, ct3, ALU.mult)
                v.tensor_tensor(m2, x2, st3, ALU.mult)
                v.tensor_tensor(m3, x1, st3, ALU.mult)
                v.tensor_tensor(m4, x2, ct3, ALU.mult).then_inc("cv", 1)
                v.wait("cv", c.val("cv"))
                v.tensor_tensor(dst_get(0), m1, m2, ALU.subtract)
                v.tensor_tensor(dst_get(1), m3, m4, ALU.add).then_inc("cv", 1)

            if STAGE in (96, 97, 98):
                lt0 = stage_WT(WA["Wk"], wtA, wstgF, wstg, 1)
                if STAGE >= 97:
                    t_.wait("lt", lt0)
                    if P97 == "full":
                        proj_tile(0, 1, 0)
                    elif P97 == "one":
                        t_.matmul(psum[:, 0:512], fmA[:, 0:128], wtA[:, 0:512],
                                  start=True, stop=True)
                        t_.matmul(psum[:, 512:1024], fmA[:, 0:128], wtA[:, 512:1024],
                                  start=True, stop=True).then_inc("ct", 1)
                    elif P97 == "acc8":
                        ins = None
                        for kc in range(NC):
                            for b_ in range(2):
                                ins = t_.matmul(psum[:, 512 * b_:512 * (b_ + 1)],
                                          fmA[:, 2048 * kc:2048 * kc + 128],
                                          wtA[:, 1024 * kc + 512 * b_:1024 * kc + 512 * (b_ + 1)],
                                          start=(kc == 0), stop=(kc == NC - 1))
                        ins.then_inc("ct", 1)
                    v.wait("ct", c.val("ct"))
                    if STAGE >= 98:
                        def dstb98(half):
                            return btmp[:].rearrange("p (h e) -> p h e", h=16)[
                                :, :, 32 * half:32 * (half + 1)]
                        rope_tile(0, t_rc, t_rs, dstb98, 0)
                    else:
                        v.tensor_copy(btmp[:], psum[:, 0:1024]).then_inc("cv", 1)
                finale()
                return
            def rope_from_sbuf(i, src_ap, c_t, s_t, dst_get):
                sr3 = src_ap.rearrange("p (h j two) -> p h (j two)", h=16, two=2)
                x1 = sr3[:, :, 0::2]
                x2 = sr3[:, :, 1::2]
                ct3 = c_t[:, 32 * i:32 * (i + 1)].unsqueeze(1).broadcast_to(
                    [128, 16, 32])
                st3 = s_t[:, 32 * i:32 * (i + 1)].unsqueeze(1).broadcast_to(
                    [128, 16, 32])
                m1 = ttmp[:, 0:512].rearrange("p (h j) -> p h j", h=16)
                m2 = ttmp[:, 512:1024].rearrange("p (h j) -> p h j", h=16)
                m3 = ttmp[:, 1024:1536].rearrange("p (h j) -> p h j", h=16)
                m4 = ttmp[:, 1536:2048].rearrange("p (h j) -> p h j", h=16)
                v.tensor_tensor(m1, x1, ct3, ALU.mult)
                v.tensor_tensor(m2, x2, st3, ALU.mult)
                v.tensor_tensor(m3, x1, st3, ALU.mult)
                v.tensor_tensor(m4, x2, ct3, ALU.mult).then_inc("cv", 1)
                v.wait("cv", c.val("cv"))
                v.tensor_tensor(dst_get(0), m1, m2, ALU.subtract)
                v.tensor_tensor(dst_get(1), m3, m4, ALU.add).then_inc("cv", 1)

            t_.wait("cv", setup_cv)
            for proj in ("k", "v", "q"):
                wsrc = WA["W" + proj]
                bias_slot = {"q": 0, "k": 1, "v": 2}[proj]
                lt0 = stage_WT(wsrc, wtA, wstgF, wstg, 1)
                t_.wait("lt", lt0)
                rope_cv = [0] * NT
                tpose_lt = [0] * NT
                for i in range(NT):
                    pb = i % 2
                    t_.wait("cv", rope_cv[i - 2] if i >= 2 else 0)
                    proj_tile(i, bias_slot, pb)
                    v.wait("ct", c.val("ct"))
                    if proj == "v":
                        def dstv(half, i=i):
                            return vtt[:, 1040 * i:1040 * i + 1040].rearrange(
                                "p (h e) -> p h e", h=16)[:, :, 32 * half:32 * (half + 1)]
                        rope_tile(i, t_rc, t_rs, dstv, pb)
                        rope_cv[i] = c.val("cv")
                        if dbg:
                            y.wait("cv", c.val("cv"))
                            y.dma_start(out=dbg_t["d_v"][128 * i:128 * (i + 1), :],
                                        in_=vtt[:, 1040 * i:1040 * (i + 1)]
                                        ).then_inc("so", 16)
                    else:
                        ctab = t_rc8 if proj == "q" else t_rc
                        stab = t_rs8 if proj == "q" else t_rs

                        def dstb(half):
                            return btmp[:].rearrange("p (h e) -> p h e", h=16)[
                                :, :, 32 * half:32 * (half + 1)]
                        v.wait("lt", tpose_lt[i - 1] if i >= 1 else 0)
                        if dbg:
                            v.wait("so", c.val("so"))
                        rope_tile(i, ctab, stab, dstb, pb)
                        rope_cv[i] = c.val("cv")
                        dstfm = qfm if proj == "q" else kfm
                        s.wait("cv", c.val("cv"))
                        if dbg:
                            y.wait("cv", c.val("cv"))
                            y.dma_start(out=dbg_t["d_" + proj][128 * i:128 * (i + 1), :],
                                        in_=btmp[:]).then_inc("so", 16)
                        for rc_ in range(NC):
                            dmaT(btmp[:, 128 * rc_:128 * (rc_ + 1)],
                                 dstfm[:, 2048 * rc_ + 128 * i:
                                       2048 * rc_ + 128 * (i + 1)])
                        tpose_lt[i] = c.val("lt")
            projA_cv = c.val("cv")
            projA_lt = c.val("lt")
            if STAGE < 3:
                finale()
                return

            v.memset(rb[:, :], 0.0).then_inc("cv", 1)
            wo_lt = stage_WT(WA["Wo"], WoT, wstgF2, wstg2, 1)
            if dbg:
                y.wait("lt", wo_lt)
                y.dma_start(out=dbg_t["d_wot"][:, :], in_=WoT[:, :]).then_inc("so", 16)
            groups = [list(range(3 * g_, min(3 * g_ + 3, 16))) for g_ in range(6)]
            exp_cs = {0: 0, 1: 0}
            exp_of_group = {}
            pv_clear_cs = 0
            pt_free_ct = 0
            pv_clear_cv = 0
            o_tt_lt = 0
            oTf_lt = {0: 0, 1: 0}
            wo_ct = 0
            rope_cv = 0
            for qb in range(QB_LIMIT):
                for h in range(16):
                    kbase = 64 * (h % 2)
                    cbase = 2048 * (h // 2)
                    for gi in range(len(groups) + 1):
                        if gi < len(groups):
                            grp = groups[gi]
                            par = gi % 2
                            bank0 = 1 if par == 0 else 4
                            t_.wait("cs", exp_cs[par])
                            if gi == 0:
                                t_.wait("lt", wo_lt if (qb == 0 and h == 0) else 0)
                                t_.wait("ct", pt_free_ct)
                                if h == 0:
                                    t_.wait("cv", c.val("cv"))
                            ins = None
                            for j, ki in enumerate(grp):
                                ins = t_.matmul(
                                    psum[:, 512 * (bank0 + j):512 * (bank0 + j + 1)],
                                    kfm[kbase:kbase + 64,
                                        cbase + 128 * ki:cbase + 128 * (ki + 1)],
                                    qfm[kbase:kbase + 64,
                                        cbase + 512 * qb:cbase + 512 * (qb + 1)],
                                    start=True, stop=True)
                            ins.then_inc("ct", 1)
                            s.wait("ct", c.val("ct"))
                            s.activation(ptA[:, 512 * grp[0]:512 * (grp[-1] + 1)],
                                         psum[:, 512 * bank0:
                                              512 * (bank0 + len(grp))],
                                         ACTF.Exp, bias=cst[:, 0:1]).then_inc("cs", 1)
                            exp_cs[par] = c.val("cs")
                            exp_of_group[gi] = c.val("cs")
                        if gi >= 1:
                            grpP = groups[gi - 1]
                            t_.wait("cs", exp_of_group[gi - 1])
                            if gi == 1:
                                t_.wait("cv", pv_clear_cv)
                            ins = None
                            for ki in grpP:
                                ins = t_.matmul(
                                    psum[0:65, 0:512],
                                    vtt[:, 1040 * ki + 65 * h:1040 * ki + 65 * (h + 1)],
                                    ptA[:, 512 * ki:512 * (ki + 1)],
                                    start=(ki == 0), stop=(ki == 15))
                            if grpP[-1] == 15:
                                ins.then_inc("ct", 1)
                    pt_free_ct = c.val("ct")
                    s.wait("ct", c.val("ct"))
                    s.activation(rb[64:65, 0:512], psum[64:65, 0:512],
                                 ACTF.Ln).then_inc("cs", 1)
                    s.wait("cs", c.val("cs"))
                    s.activation(rb[64:65, 0:512], rb[64:65, 0:512], ACTF.Exp,
                                 scale=-1.0).then_inc("cs", 1)
                    t_.wait("cs", c.val("cs"))
                    t_.matmul(psum[:, 3584:4096], e65[:, :], rb[:, 0:512],
                              start=True, stop=True).then_inc("ct", 1)
                    v.wait("ct", c.val("ct"))
                    v.tensor_copy(rsb[:, 0:512], psum[:, 3584:4096]).then_inc("cv", 1)
                    v.wait("cv", c.val("cv"))
                    v.wait("lt", oTf_lt[h % 2])
                    ob = 64 * (h % 2)
                    v.tensor_tensor(oTf[ob:ob + 64, 0:512], psum[0:64, 0:512],
                                    rsb[0:64, 0:512], ALU.mult).then_inc("cv", 1)
                    pv_clear_cs = 0
                    pv_clear_cv = c.val("cv")
                    s.wait("cv", c.val("cv"))
                    if h == 0:
                        s.wait("cv", rope_cv)
                    for qs in range(4):
                        dmaT(oTf[ob:ob + 64, 128 * qs:128 * (qs + 1)],
                             o_tt[:, 1024 * qs + 64 * h:1024 * qs + 64 * (h + 1)])
                    oTf_lt[h % 2] = c.val("lt")
                    o_tt_lt = c.val("lt")
                v.wait("lt", o_tt_lt)
                onorm_cv = c.val("cv")
                if dbg:
                    y.wait("lt", o_tt_lt)
                    for qs in range(4):
                        y.dma_start(out=dbg_t["d_ott"][128 * (4 * qb + qs):
                                                       128 * (4 * qb + qs + 1), :],
                                    in_=o_tt[:, 1024 * qs:1024 * (qs + 1)]
                                    ).then_inc("so", 16)
                for qs in range(4):
                    i = 4 * qb + qs
                    v.wait("lt", o_tt_lt)
                    v.wait("lt", c.val("lt"))
                    if dbg:
                        v.wait("so", c.val("so"))

                    def dsto(half, qs=qs):
                        return btmp[:].rearrange("p (h e) -> p h e", h=16)[
                            :, :, 32 * half:32 * (half + 1)]
                    rope_from_sbuf(i, o_tt[:, 1024 * qs:1024 * (qs + 1)],
                                   t_rc, t_rs, dsto)
                    s.wait("cv", c.val("cv"))
                    if dbg:
                        y.wait("cv", c.val("cv"))
                        y.dma_start(out=dbg_t["d_h2"][128 * i:128 * (i + 1), :],
                                    in_=btmp[:]).then_inc("so", 16)
                        s.wait("so", c.val("so"))
                    if qs == 0:
                        s.wait("ct", wo_ct)
                    for rc_ in range(NC):
                        dmaT(btmp[:, 128 * rc_:128 * (rc_ + 1)],
                             o2fm[:, 512 * rc_ + 128 * qs:512 * rc_ + 128 * (qs + 1)])
                if dbg and qb == 0:
                    y.wait("lt", c.val("lt"))
                    y.dma_start(out=dbg_t["d_o2fm"][:, :],
                                in_=o2fm[:, :]).then_inc("so", 16)
                t_.wait("lt", c.val("lt"))
                if dbg and qb == 0:
                    t_.wait("so", c.val("so"))
                for qs in range(4):
                    i = 4 * qb + qs
                    t_.wait("cv", c.val("cv"))
                    ins = None
                    for rc_ in range(NC):
                        for b_ in range(2):
                            ins = t_.matmul(
                                psum[:, 512 * (1 + b_):512 * (2 + b_)],
                                o2fm[:, 512 * rc_ + 128 * qs:512 * rc_ + 128 * (qs + 1)],
                                WoT[:, 1024 * rc_ + 512 * b_:1024 * rc_ + 512 * (b_ + 1)],
                                start=(rc_ == 0), stop=False)
                    for b_ in range(2):
                        ins = t_.matmul(psum[:, 512 * (1 + b_):512 * (2 + b_)],
                                        e1[:, :],
                                        rows_bf[:, 3 * 1024 + 512 * b_:
                                                3 * 1024 + 512 * (b_ + 1)],
                                        start=False, stop=True)
                    ins.then_inc("ct", 1)
                    if dbg:
                        v.wait("ct", c.val("ct"))
                        v.tensor_copy(ttmp[:, 1024:2048],
                                      psum[:, 512:1536]).then_inc("cv", 1)
                        y.wait("cv", c.val("cv"))
                        y.dma_start(out=dbg_t["d_xf1"][128 * i:128 * (i + 1), :],
                                    in_=ttmp[:, 1024:2048]).then_inc("so", 16)
                        v.wait("so", c.val("so"))
                    y.wait("cv", c.val("cv"))
                    y.dma_start(out=ttmp[:, 0:1024],
                                in_=x_in[128 * i:128 * (i + 1), :]).then_inc("ld", 16)
                    v.wait("ld", c.val("ld"))
                    v.wait("ct", c.val("ct"))
                    v.tensor_sub(ttmp[:, 1024:2048], ttmp[:, 0:1024],
                                 psum[:, 512:1536]).then_inc("cv", 1)
                    y.wait("cv", c.val("cv"))
                    y.dma_start(out=xs[128 * i:128 * (i + 1), :],
                                in_=ttmp[:, 1024:2048]).then_inc("ld", 16)
                    if dbg:
                        y.dma_start(out=dbg_t["d_s1"][128 * i:128 * (i + 1), :],
                                    in_=ttmp[:, 1024:2048]).then_inc("so", 16)
                    y.wait("ld", c.val("ld"))
                    if dbg:
                        y.wait("so", c.val("so"))
                wo_ct = c.val("ct")
                rope_cv = c.val("cv")
            if STAGE < 4:
                finale()
                return
            phase_B(c)
            finale()
            return

        def phase_B(c):
            v, s, t_, g, y = c.v, c.s, c.t, c.g, c.y

            def dmaT(src_ap, dst_ap):
                s.dma_start(out=dst_ap, in_=src_ap,
                            transpose=True).then_inc("lt", 16)

            def bar():
                for e_ in (y, v, s, t_, g):
                    for sem in ("ld", "lg", "lt", "so", "cv", "cs", "ct", "cg"):
                        e_.wait(sem, c.val(sem))

            def preload_set(setname):
                if c.active != "scalar":
                    return
                from concourse.hw_specs import get_activation_tables
                eng = c.eng["scalar"]
                for idx, nm in enumerate(get_activation_tables(nc.m.arch)):
                    if nm == setname:
                        eng.add_instruction(mybir.InstLoadActFuncSet(
                            name=nc.get_next_instruction_name(),
                            act_func_set_id=idx, ins=[], outs=[]))
                        return
                raise ValueError(setname)

            bar()
            v.memset(rb[:, :], 0.0).then_inc("cv", 1)
            apply_cv = [0] * NT
            for i in range(NT):
                sl = (i % 2) * 1024
                y.wait("cv", apply_cv[i - 2] if i >= 2 else 0)
                y.wait("ld", c.val("ld"))
                y.dma_start(out=ttmp[:, sl:sl + 1024],
                            in_=xs[128 * i:128 * (i + 1), :]).then_inc("ld", 16)
                v.wait("ld", c.val("ld"))
                v.bn_stats(stat[:, 0:6], ttmp[:, sl:sl + 512])
                v.bn_stats(stat[:, 6:12], ttmp[:, sl + 512:sl + 1024]).then_inc("cv", 1)
                v.wait("cv", c.val("cv"))
                v.bn_aggr(mv[:, 2 * i:2 * i + 2], stat[:, 0:12]).then_inc("cv", 1)
                v.wait("cv", c.val("cv"))
                v.tensor_scalar_add(rst[:, i:i + 1], mv[:, 2 * i + 1:2 * i + 2],
                                    cst[:, 1:2]).then_inc("cv", 1)
                s.wait("cv", c.val("cv"))
                s.activation(rst[:, i:i + 1], rst[:, i:i + 1],
                             ACTF.Ln).then_inc("cs", 1)
                s.wait("cs", c.val("cs"))
                s.activation(rst[:, i:i + 1], rst[:, i:i + 1], ACTF.Exp,
                             scale=-0.5).then_inc("cs", 1)
                v.wait("cs", c.val("cs"))
                v.tensor_scalar(ttm2[:], ttmp[:, sl:sl + 1024],
                                mv[:, 2 * i:2 * i + 1], rst[:, i:i + 1],
                                ALU.subtract, ALU.mult).then_inc("cv", 1)
                apply_cv[i] = c.val("cv")
                y.wait("cv", c.val("cv"))
                y.dma_start(out=xfs[128 * i:128 * (i + 1), :],
                            in_=ttm2[:]).then_inc("ld", 16)
                if dbg:
                    y.dma_start(out=dbg_t["d_xf1"][128 * i:128 * (i + 1), :],
                                in_=ttm2[:]).then_inc("so", 16)
                v.wait("lt", c.val("lt"))
                v.tensor_copy(btmp[:], ttm2[:]).then_inc("cv", 1)
                apply_cv[i] = c.val("cv")
                s.wait("cv", c.val("cv"))
                for cc in range(NC):
                    dmaT(btmp[:, 128 * cc:128 * (cc + 1)],
                         fmB[:, FMS * cc + 16 + 128 * i:FMS * cc + 16 + 128 * (i + 1)])
            v.wait("lt", c.val("lt"))
            z = None
            for cc in range(NC):
                z = v.memset(fmB[:, FMS * cc + 15:FMS * cc + 16], 0.0)
            z.then_inc("cv", 1)
            fmB_cv = c.val("cv")
            fmB_lt = c.val("lt")

            for l in range(2):
                bar()
                v.memset(ptB[:, :], 0.0)
                v.memset(vttL[:, :], 1.0).then_inc("cv", 1)
                if l == 1:
                    z = None
                    for cc in range(NC):
                        z = v.memset(fmB[:, FMS * cc:FMS * cc + 16], 0.0)
                    z.then_inc("cv", 1)
                    reb_cv = [0] * NT
                    for i in range(NT):
                        sl = (i % 2) * 1024
                        y.wait("cv", reb_cv[i - 2] if i >= 2 else c.val("cv"))
                        y.wait("ld", c.val("ld"))
                        y.dma_start(out=ttmp[:, sl:sl + 1024],
                                    in_=xfs[128 * i:128 * (i + 1), :]).then_inc("ld", 16)
                        v.wait("ld", c.val("ld"))
                        v.wait("lt", c.val("lt"))
                        v.tensor_copy(btmp[:], ttmp[:, sl:sl + 1024]).then_inc("cv", 1)
                        reb_cv[i] = c.val("cv")
                        s.wait("cv", c.val("cv"))
                        for cc in range(NC):
                            dmaT(btmp[:, 128 * cc:128 * (cc + 1)],
                                 fmB[:, FMS * cc + 16 + 128 * i:
                                     FMS * cc + 16 + 128 * (i + 1)])
                    fmB_cv = c.val("cv")
                    fmB_lt = c.val("lt")

                for q4 in range(4):
                    y.wait("cv", c.val("cv"))
                    y.dma_start(
                        out=tscr[:, 0:3072].rearrange("p (kc f) -> p kc f", kc=2),
                        in_=wqkv[l][:, :].rearrange("(kc p) f -> p kc f", p=128)[
                            :, 2 * q4:2 * q4 + 2, :]).then_inc("ld", 16)
                    v.wait("ld", c.val("ld"))
                    ins = None
                    for kc2 in range(2):
                        kc = 2 * q4 + kc2
                        ins = v.tensor_copy(wkv[:, 1536 * kc:1536 * (kc + 1)],
                                            tscr[:, 1536 * kc2:1536 * (kc2 + 1)])
                    ins.then_inc("cv", 1)
                wkv_cv = c.val("cv")
                for q2 in range(2):
                    y.wait("cv", c.val("cv"))
                    y.dma_start(
                        out=tscr[:, 0:2048].rearrange("p (kc f) -> p kc f", kc=2),
                        in_=wol[l][:, :].rearrange("(kc p) f -> p kc f", p=128)[
                            :, 2 * q2:2 * q2 + 2, :]).then_inc("ld", 16)
                    v.wait("ld", c.val("ld"))
                    v.tensor_copy(woS[:, 2048 * q2:2048 * (q2 + 1)],
                                  tscr[:, 0:2048]).then_inc("cv", 1)
                for idx, (srcrow, scl) in enumerate([(qsc[l], 8.0), (ksc[l], 1.0)]):
                    y.wait("cv", c.val("cv"))
                    y.dma_start(out=tscr[64:65, 0:64], in_=srcrow[:, :]).then_inc("ld", 16)
                    v.wait("ld", c.val("ld"))
                    v.memset(rb[:, 0:64], 0.0).then_inc("cv", 1)
                    v.wait("cv", c.val("cv"))
                    v.tensor_copy(rb[64:65, 0:64], tscr[64:65, 0:64]).then_inc("cv", 1)
                    t_.wait("cv", c.val("cv"))
                    t_.matmul(psum[:, 3072:3136], e65[:, :], rb[:, 0:64],
                              start=True, stop=True).then_inc("ct", 1)
                    v.wait("ct", c.val("ct"))
                    v.tensor_scalar_mul(hrep[:, 64 * idx:64 * (idx + 1)],
                                        psum[:, 3072:3136], scl).then_inc("cv", 1)
                hrep_cv = c.val("cv")
                if STAGE < 5:
                    continue

                t_.wait("cv", max(wkv_cv, fmB_cv, hrep_cv))
                t_.wait("lt", fmB_lt)
                prep_cv = [0] * NT
                tp_lt = [0] * NT
                for i in range(NT):
                    pb = i % 2
                    po = 1536 * pb
                    t_.wait("cv", prep_cv[i - 2] if i >= 2 else 0)
                    ins = None
                    for kc in range(NC):
                        off = FMS * kc + (16 if kc < 4 else 15) + 128 * i
                        lhsT = fmB[:, off:off + 128]
                        for b_ in range(3):
                            ins = t_.matmul(
                                psum[:, po + 512 * b_:po + 512 * (b_ + 1)],
                                lhsT, wkv[:, 1536 * kc + 512 * b_:
                                          1536 * kc + 512 * (b_ + 1)],
                                start=(kc == 0), stop=(kc == NC - 1))
                    ins.then_inc("ct", 1)
                    s.wait("ct", c.val("ct"))
                    s.copy(tscr[:, 0:512], psum[:, po:po + 512])
                    s.copy(tscr[:, 512:1024], psum[:, po + 512:po + 1024]).then_inc("cs", 1)
                    v.wait("ct", c.val("ct"))
                    v3 = vttL[:, 520 * i:520 * (i + 1)].rearrange(
                        "p (h e) -> p h e", h=8)[:, :, 0:64]
                    v.tensor_copy(v3, psum[:, po + 1024:po + 1536].rearrange(
                        "p (h d) -> p h d", h=8)).then_inc("cv", 1)
                    prep_cv[i] = c.val("cv")
                    v.wait("cs", c.val("cs"))
                    v.tensor_tensor(tscr[:, 1024:1536], tscr[:, 0:512],
                                    tscr[:, 0:512], ALU.mult)
                    v.tensor_tensor(tscr[:, 1536:2048], tscr[:, 512:1024],
                                    tscr[:, 512:1024], ALU.mult).then_inc("cv", 1)
                    v.wait("cv", c.val("cv"))
                    v.tensor_reduce(sm1[:, 0:8].rearrange("p (h o) -> p h o", h=8),
                                    tscr[:, 1024:1536].rearrange("p (h d) -> p h d", h=8),
                                    axis=AX.X, op=ALU.add)
                    v.tensor_reduce(sm1[:, 8:16].rearrange("p (h o) -> p h o", h=8),
                                    tscr[:, 1536:2048].rearrange("p (h d) -> p h d", h=8),
                                    axis=AX.X, op=ALU.add).then_inc("cv", 1)
                    v.wait("cv", c.val("cv"))
                    v.tensor_scalar_max(sm1[:, 16:32], sm1[:, 0:16],
                                        cst[:, 2:3]).then_inc("cv", 1)
                    s.wait("cv", c.val("cv"))
                    s.activation(sm1[:, 16:32], sm1[:, 16:32],
                                 ACTF.Ln).then_inc("cs", 1)
                    s.wait("cs", c.val("cs"))
                    s.activation(sm1[:, 16:32], sm1[:, 16:32], ACTF.Exp,
                                 scale=-0.5).then_inc("cs", 1)
                    v.wait("cs", c.val("cs"))
                    ins = None
                    for hh in range(8):
                        ins = v.tensor_scalar_mul(
                            tscr[:, 64 * hh:64 * (hh + 1)],
                            tscr[:, 64 * hh:64 * (hh + 1)],
                            sm1[:, 16 + hh:17 + hh])
                        ins = v.tensor_scalar_mul(
                            tscr[:, 512 + 64 * hh:512 + 64 * (hh + 1)],
                            tscr[:, 512 + 64 * hh:512 + 64 * (hh + 1)],
                            sm1[:, 24 + hh:25 + hh])
                    ins.then_inc("cv", 1)
                    v.wait("cv", c.val("cv"))
                    q3 = tscr[:, 0:512].rearrange("p (h d) -> p h d", h=8)
                    k3 = tscr[:, 512:1024].rearrange("p (h d) -> p h d", h=8)
                    v.tensor_tensor(q3, q3, hrep[:, 0:64].unsqueeze(1).broadcast_to(
                        [128, 8, 64]), ALU.mult)
                    v.tensor_tensor(k3, k3, hrep[:, 64:128].unsqueeze(1).broadcast_to(
                        [128, 8, 64]), ALU.mult).then_inc("cv", 1)
                    v.wait("cv", c.val("cv"))
                    v.wait("lt", tp_lt[i - 1] if i >= 1 else 0)
                    for (src3, tA, tB, dstb) in ((q3, t_xA, t_xB, btmp),
                                                 (k3, t_xAi, t_xBi, btmp2)):
                        a3 = tA[:, 64 * i:64 * (i + 1)].unsqueeze(1).broadcast_to(
                            [128, 8, 64])
                        b3 = tB[:, 64 * i:64 * (i + 1)].unsqueeze(1).broadcast_to(
                            [128, 8, 64])
                        m1 = ttmp[:, 0:512].rearrange("p (h d) -> p h d", h=8)
                        m2 = ttmp[:, 512:1024].rearrange("p (h d) -> p h d", h=8)
                        v.tensor_tensor(m1, src3, a3, ALU.mult)
                        v.tensor_tensor(m2, src3, b3, ALU.mult).then_inc("cv", 1)
                        v.wait("cv", c.val("cv"))
                        d3 = dstb[:, 0:512].rearrange("p (h d) -> p h d", h=8)
                        v.tensor_tensor(d3[:, :, 0:32], m1[:, :, 0:32],
                                        m2[:, :, 32:64], ALU.subtract)
                        v.tensor_tensor(d3[:, :, 32:64], m1[:, :, 32:64],
                                        m2[:, :, 0:32], ALU.add).then_inc("cv", 1)
                    prep_cv[i] = c.val("cv")
                    if dbg and l == 0:
                        y.wait("cv", c.val("cv"))
                        y.dma_start(out=dbg_t["d_fq"][128 * i:128 * (i + 1), :],
                                    in_=btmp[:, 0:512]).then_inc("so", 16)
                        y.dma_start(out=dbg_t["d_fk"][128 * i:128 * (i + 1), :],
                                    in_=btmp2[:, 0:512]).then_inc("so", 16)
                        y.dma_start(out=dbg_t["d_fv"][128 * i:128 * (i + 1), :],
                                    in_=vttL[:, 520 * i:520 * (i + 1)]).then_inc("so", 16)
                        s.wait("so", c.val("so"))
                    s.wait("cv", c.val("cv"))
                    for rc_ in range(4):
                        dmaT(btmp[:, 128 * rc_:128 * (rc_ + 1)],
                             qfmL[:, 2048 * rc_ + 128 * i:2048 * rc_ + 128 * (i + 1)])
                        dmaT(btmp2[:, 128 * rc_:128 * (rc_ + 1)],
                             kfmL[:, 2048 * rc_ + 128 * i:2048 * rc_ + 128 * (i + 1)])
                    tp_lt[i] = c.val("lt")
                prepB_lt = c.val("lt")
                if STAGE < 6:
                    continue

                t_.wait("lt", prepB_lt)
                exp_done_cs = 0
                ptb_free_ct = {0: 0, 1: 0}
                oTfB_lt = {0: 0, 1: 0}
                pvB_clear_cv = 0
                o3tt_lt = 0
                o3_rd_cv = 0
                woB_ct = 0
                uidx = 0
                for w in range(4):
                    for h in range(HL):
                        par = uidx % 2
                        pto = 4096 * par
                        kb = 64 * (h % 2)
                        cb = 2048 * (h // 2)
                        first_c = 4 if w == 0 else 0
                        for half in range(2):
                            cr = [cc for cc in range(4 * half, 4 * half + 4)
                                  if cc >= first_c]
                            if not cr:
                                continue
                            t_.wait("cs", exp_done_cs)
                            if half == 0 or w == 0:
                                t_.wait("ct", ptb_free_ct[par])
                            ins = None
                            for cc in cr:
                                vt = (w - 1) * 4 + cc
                                alo, ahi = max(0, cc - 4), min(3, cc)
                                ins = t_.matmul(
                                    psum[:, 512 * (cc % 4) + 128 * alo:
                                         512 * (cc % 4) + 128 * (ahi + 1)],
                                    kfmL[kb:kb + 64, cb + 128 * vt:cb + 128 * (vt + 1)],
                                    qfmL[kb:kb + 64, cb + 512 * w + 128 * alo:
                                         cb + 512 * w + 128 * (ahi + 1)],
                                    start=True, stop=True)
                            ins.then_inc("ct", 1)
                            v.wait("ct", c.val("ct"))
                            blk = psum[:, 0:2560].rearrange(
                                "p (b e) -> p b e", b=4)[:, :, 0:128]
                            mt = (t_md if half == 1 else t_ma)[:].rearrange(
                                "p (b e) -> p b e", b=4)
                            v.tensor_tensor(blk, blk, mt, ALU.add).then_inc("cv", 1)
                            s.wait("cv", c.val("cv"))
                            ins = None
                            for cc in cr:
                                alo, ahi = max(0, cc - 4), min(3, cc)
                                ins = s.activation(
                                    ptB[:, pto + 512 * cc + 128 * alo:
                                        pto + 512 * cc + 128 * (ahi + 1)],
                                    psum[:, 512 * (cc % 4) + 128 * alo:
                                         512 * (cc % 4) + 128 * (ahi + 1)],
                                    ACTF.Exp, bias=cst[:, 0:1])
                            ins.then_inc("cs", 1)
                            exp_done_cs = c.val("cs")
                            t_.wait("cs", exp_done_cs)
                            if half == 0 or w == 0:
                                t_.wait("cv", pvB_clear_cv)
                            ins = None
                            for cc in cr:
                                vt = (w - 1) * 4 + cc
                                ins = t_.matmul(
                                    psum[0:65, 3584:4096],
                                    vttL[:, 520 * vt + 65 * h:520 * vt + 65 * (h + 1)],
                                    ptB[:, pto + 512 * cc:pto + 512 * (cc + 1)],
                                    start=(cc == first_c), stop=(cc == 7))
                            if cr[-1] == 7:
                                ins.then_inc("ct", 1)
                        ptb_free_ct[par] = c.val("ct")
                        s.wait("ct", c.val("ct"))
                        s.activation(rb[64:65, 0:512], psum[64:65, 3584:4096],
                                     ACTF.Ln).then_inc("cs", 1)
                        s.wait("cs", c.val("cs"))
                        s.activation(rb[64:65, 0:512], rb[64:65, 0:512], ACTF.Exp,
                                     scale=-1.0).then_inc("cs", 1)
                        t_.wait("cs", c.val("cs"))
                        t_.matmul(psum[:, 3072:3584], e65[:, :], rb[:, 0:512],
                                  start=True, stop=True).then_inc("ct", 1)
                        v.wait("ct", c.val("ct"))
                        v.tensor_copy(rsb[:, 0:512], psum[:, 3072:3584]).then_inc("cv", 1)
                        v.wait("cv", c.val("cv"))
                        v.wait("lt", oTfB_lt[h % 2])
                        ob = 64 * (h % 2)
                        v.tensor_tensor(oTfB[ob:ob + 64, 0:512],
                                        psum[0:64, 3584:4096],
                                        rsb[0:64, 0:512], ALU.mult).then_inc("cv", 1)
                        pvB_clear_cv = c.val("cv")
                        s.wait("cv", c.val("cv"))
                        if h == 0:
                            s.wait("cv", o3_rd_cv)
                        for sub in range(4):
                            dmaT(oTfB[ob:ob + 64, 128 * sub:128 * (sub + 1)],
                                 o3tt[:, 512 * sub + 64 * h:512 * sub + 64 * (h + 1)])
                        oTfB_lt[h % 2] = c.val("lt")
                        o3tt_lt = c.val("lt")
                        uidx += 1
                    if dbg and l == 0:
                        y.wait("lt", o3tt_lt)
                        for sub in range(4):
                            y.dma_start(
                                out=dbg_t["d_fott"][128 * (4 * w + sub):
                                                    128 * (4 * w + sub + 1), :],
                                in_=o3tt[:, 512 * sub:512 * (sub + 1)]).then_inc("so", 16)
                        s.wait("so", c.val("so"))
                    s.wait("lt", o3tt_lt)
                    s.wait("ct", woB_ct)
                    for sub in range(4):
                        for rc_ in range(4):
                            dmaT(o3tt[:, 512 * sub + 128 * rc_:
                                      512 * sub + 128 * (rc_ + 1)],
                                 o3fm[:, 512 * rc_ + 128 * sub:
                                      512 * rc_ + 128 * (sub + 1)])
                    o3fm_lt = c.val("lt")
                    t_.wait("lt", o3fm_lt)
                    for sub in range(4):
                        i = 4 * w + sub
                        t_.wait("cv", c.val("cv"))
                        ins = None
                        for rc_ in range(4):
                            for b_ in range(2):
                                ins = t_.matmul(
                                    psum[:, 2048 + 512 * b_:2048 + 512 * (b_ + 1)],
                                    o3fm[:, 512 * rc_ + 128 * sub:
                                         512 * rc_ + 128 * (sub + 1)],
                                    woS[:, 1024 * rc_ + 512 * b_:
                                        1024 * rc_ + 512 * (b_ + 1)],
                                    start=(rc_ == 0), stop=(rc_ == 3))
                        ins.then_inc("ct", 1)
                        y.wait("cv", c.val("cv"))
                        y.wait("ld", c.val("ld"))
                        y.dma_start(out=ttmp[:, 0:1024],
                                    in_=xfs[128 * i:128 * (i + 1), :]).then_inc("ld", 16)
                        v.wait("ld", c.val("ld"))
                        v.wait("ct", c.val("ct"))
                        v.tensor_add(ttmp[:, 1024:2048], ttmp[:, 0:1024],
                                     psum[:, 2048:3072]).then_inc("cv", 1)
                        y.wait("cv", c.val("cv"))
                        y.dma_start(out=xfs[128 * i:128 * (i + 1), :],
                                    in_=ttmp[:, 1024:2048]).then_inc("ld", 16)
                        y.wait("ld", c.val("ld"))
                    woB_ct = c.val("ct")
                    o3_rd_cv = c.val("cv")
                if STAGE < 7:
                    continue

                bar()
                for kc in range(W1C):
                    rows = 128 if kc < W1C - 1 else LAST
                    y.wait("cv", c.val("cv"))
                    y.dma_start(out=w1f[0:rows, 0:1024],
                                in_=w2[l][128 * kc:128 * kc + rows, :]).then_inc("ld", 16)
                    v.wait("ld", c.val("ld"))
                    v.tensor_copy(w2s[0:rows, 1024 * kc:1024 * (kc + 1)],
                                  w1f[0:rows, 0:1024]).then_inc("cv", 1)
                y.wait("cv", c.val("cv"))
                y.dma_start(out=w1f[0:1, 0:1024], in_=b2[l][:, :]).then_inc("ld", 16)
                v.wait("ld", c.val("ld"))
                v.tensor_copy(rows_bf[0:1, 0:1024], w1f[0:1, 0:1024]).then_inc("cv", 1)
                slabs = [(lng[l], 0, 8, 0), (lnb[l], 0, 8, 8),
                         (b1[l], 0, 22, 16), (b1[l], HID, 22, 38)]
                for srct, off0, prows, bcol in slabs:
                    y.wait("cv", c.val("cv"))
                    if prows == 8:
                        y.dma_start(out=w1f[0:8, 0:128],
                                    in_=srct[:, :].rearrange("o (c f) -> (o c) f", c=8)
                                    ).then_inc("ld", 16)
                    else:
                        y.dma_start(out=w1f[0:21, 0:128],
                                    in_=srct[0:1, off0:off0 + 2688].rearrange(
                                        "o (c f) -> (o c) f", c=21)).then_inc("ld", 16)
                        y.dma_start(out=w1f[21:22, 0:LAST],
                                    in_=srct[0:1, off0 + 2688:off0 + HID]).then_inc("ld", 16)
                    t_.wait("ld", c.val("ld"))
                    t_.transpose(psum[0:128, 3072:3072 + prows],
                                 w1f[0:prows, 0:128],
                                 idf[0:prows, 0:prows]).then_inc("ct", 1)
                    v.wait("ct", c.val("ct"))
                    v.tensor_copy(bfm[:, bcol:bcol + prows],
                                  psum[:, 3072:3072 + prows]).then_inc("cv", 1)
                bfm_cv = c.val("cv")

                for hf in range(2):
                    if USE_PRELOAD:
                        preload_set("natural_log_exp_and_others")
                    ln2_cv = [0] * 8
                    for k in range(8):
                        ti = 8 * hf + k
                        sl = (k % 2) * 1024
                        y.wait("cv", ln2_cv[k - 2] if k >= 2 else c.val("cv"))
                        y.wait("ld", c.val("ld"))
                        y.dma_start(out=ttmp[:, sl:sl + 512],
                                    in_=xfs[128 * ti:128 * (ti + 1), 0:512]
                                    ).then_inc("ld", 16)
                        if ti == 0:
                            y.dma_start(out=ttmp[1:128, sl + 512:sl + 1024],
                                        in_=xfs[0:127, 512:1024]).then_inc("ld", 16)
                        else:
                            y.dma_start(out=ttmp[:, sl + 512:sl + 1024],
                                        in_=xfs[128 * ti - 1:128 * (ti + 1) - 1,
                                                512:1024]).then_inc("ld", 16)
                        v.wait("ld", c.val("ld"))
                        if ti == 0:
                            v.memset(ttmp[0:1, sl + 512:sl + 1024], 0.0)
                        v.tensor_copy(htt[:, 1024 * k:1024 * (k + 1)],
                                      ttmp[:, sl:sl + 1024]).then_inc("cv", 1)
                        v.wait("cv", c.val("cv"))
                        v.bn_stats(stat[:, 0:6], htt[:, 1024 * k:1024 * k + 512])
                        v.bn_stats(stat[:, 6:12],
                                   htt[:, 1024 * k + 512:1024 * (k + 1)]).then_inc("cv", 1)
                        v.wait("cv", c.val("cv"))
                        v.bn_aggr(mv[:, 2 * k:2 * k + 2], stat[:, 0:12]).then_inc("cv", 1)
                        v.wait("cv", c.val("cv"))
                        v.tensor_scalar_add(rst[:, k:k + 1], mv[:, 2 * k + 1:2 * k + 2],
                                            cst[:, 1:2]).then_inc("cv", 1)
                        s.wait("cv", c.val("cv"))
                        s.activation(rst[:, k:k + 1], rst[:, k:k + 1],
                                     ACTF.Ln).then_inc("cs", 1)
                        s.wait("cs", c.val("cs"))
                        s.activation(rst[:, k:k + 1], rst[:, k:k + 1], ACTF.Exp,
                                     scale=-0.5).then_inc("cs", 1)
                        v.wait("cs", c.val("cs"))
                        v.wait("lt", c.val("lt"))
                        v.tensor_scalar(btmp[:], htt[:, 1024 * k:1024 * (k + 1)],
                                        mv[:, 2 * k:2 * k + 1], rst[:, k:k + 1],
                                        ALU.subtract, ALU.mult).then_inc("cv", 1)
                        ln2_cv[k] = c.val("cv")
                        if dbg and l == 0:
                            y.wait("cv", c.val("cv"))
                            y.dma_start(out=dbg_t["d_h2"][128 * ti:128 * (ti + 1), :],
                                        in_=btmp[:]).then_inc("so", 16)
                            s.wait("so", c.val("so"))
                        s.wait("cv", c.val("cv"))
                        for cc in range(NC):
                            dmaT(btmp[:, 128 * cc:128 * (cc + 1)],
                                 xn2[:, 1024 * cc + 128 * k:1024 * cc + 128 * (k + 1)])
                    v.wait("lt", c.val("lt"))
                    v.wait("cv", bfm_cv)
                    ins = None
                    for cc in range(NC):
                        ins = v.tensor_scalar(xn2[:, 1024 * cc:1024 * (cc + 1)],
                                              xn2[:, 1024 * cc:1024 * (cc + 1)],
                                              bfm[:, cc:cc + 1], bfm[:, 8 + cc:9 + cc],
                                              ALU.mult, ALU.add)
                    ins.then_inc("cv", 1)
                    xn2_cv = c.val("cv")
                    if USE_PRELOAD:
                        s.wait("cv", xn2_cv)
                        preload_set("gelu_and_others")
                    t_.wait("cv", xn2_cv)
                    cast_cv = [0, 0]
                    gate_cv = [0, 0]
                    for cc in range(W1C):
                        rows = 128 if cc < W1C - 1 else LAST
                        par = cc % 2
                        wo_ = 2048 * par
                        y.wait("cv", cast_cv[par])
                        y.wait("ld", c.val("ld"))
                        y.dma_start(
                            out=w1f[:, wo_:wo_ + 1024].rearrange(
                                "p (kc f) -> p kc f", kc=8)[:, :, 0:rows],
                            in_=w1[l][:, 128 * cc:128 * cc + rows].rearrange(
                                "(kc p) f -> p kc f", p=128)).then_inc("ld", 16)
                        y.dma_start(
                            out=w1f[:, wo_ + 1024:wo_ + 2048].rearrange(
                                "p (kc f) -> p kc f", kc=8)[:, :, 0:rows],
                            in_=w1[l][:, HID + 128 * cc:HID + 128 * cc + rows].rearrange(
                                "(kc p) f -> p kc f", p=128)).then_inc("ld", 16)
                        v.wait("ld", c.val("ld"))
                        v.tensor_copy(w1s[:, wo_:wo_ + 2048],
                                      w1f[:, wo_:wo_ + 2048]).then_inc("cv", 1)
                        cast_cv[par] = c.val("cv")
                        t_.wait("cv", max(cast_cv[par], gate_cv[par]))
                        ins = None
                        for kc8 in range(NC):
                            for b_ in range(2):
                                ins = t_.matmul(
                                    psum[0:rows, wo_ + 512 * b_:wo_ + 512 * (b_ + 1)],
                                    w1s[:, wo_ + 128 * kc8:wo_ + 128 * kc8 + rows],
                                    xn2[:, 1024 * kc8 + 512 * b_:
                                        1024 * kc8 + 512 * (b_ + 1)],
                                    start=(kc8 == 0), stop=(kc8 == NC - 1))
                                ins = t_.matmul(
                                    psum[0:rows, wo_ + 1024 + 512 * b_:
                                         wo_ + 1024 + 512 * (b_ + 1)],
                                    w1s[:, wo_ + 1024 + 128 * kc8:
                                        wo_ + 1024 + 128 * kc8 + rows],
                                    xn2[:, 1024 * kc8 + 512 * b_:
                                        1024 * kc8 + 512 * (b_ + 1)],
                                    start=(kc8 == 0), stop=(kc8 == NC - 1))
                        ins.then_inc("ct", 1)
                        s.wait("ct", c.val("ct"))
                        s.activation(gel[0:rows, 0:1024],
                                     psum[0:rows, wo_ + 1024:wo_ + 2048],
                                     ACTF.Gelu, bias=bfm[0:rows, 38 + cc:39 + cc]
                                     ).then_inc("cs", 1)
                        v.wait("cs", c.val("cs"))
                        v.scalar_tensor_tensor(
                            wgat[0:rows, 1024 * cc:1024 * (cc + 1)],
                            psum[0:rows, wo_:wo_ + 1024],
                            bfm[0:rows, 16 + cc:17 + cc],
                            gel[0:rows, 0:1024],
                            ALU.add, ALU.mult).then_inc("cv", 1)
                        gate_cv[par] = c.val("cv")
                    gate_all_cv = c.val("cv")
                    t_.wait("cv", gate_all_cv)
                    for k in range(8):
                        ti = 8 * hf + k
                        pb = k % 2
                        wo_ = 2048 * pb
                        t_.wait("cv", c.val("cv"))
                        ins = None
                        for kc in range(W1C):
                            rows = 128 if kc < W1C - 1 else LAST
                            for b_ in range(2):
                                ins = t_.matmul(
                                    psum[:, wo_ + 512 * b_:wo_ + 512 * (b_ + 1)],
                                    wgat[0:rows, 1024 * kc + 128 * k:
                                         1024 * kc + 128 * (k + 1)],
                                    w2s[0:rows, 1024 * kc + 512 * b_:
                                        1024 * kc + 512 * (b_ + 1)],
                                    start=(kc == 0), stop=False)
                        for b_ in range(2):
                            ins = t_.matmul(
                                psum[:, wo_ + 512 * b_:wo_ + 512 * (b_ + 1)],
                                e1[:, :], rows_bf[:, 512 * b_:512 * (b_ + 1)],
                                start=False, stop=True)
                        ins.then_inc("ct", 1)
                        y.wait("cv", c.val("cv"))
                        y.wait("ld", c.val("ld"))
                        y.dma_start(out=ttmp[:, 0:1024],
                                    in_=xfs[128 * ti:128 * (ti + 1), :]).then_inc("ld", 16)
                        if l == 1:
                            y.dma_start(out=ttm2[:, 0:1024],
                                        in_=xs[128 * ti:128 * (ti + 1), :]).then_inc("ld", 16)
                        v.wait("ld", c.val("ld"))
                        v.wait("ct", c.val("ct"))
                        v.tensor_add(ttmp[:, 1024:2048], ttmp[:, 0:1024],
                                     psum[:, wo_:wo_ + 1024]).then_inc("cv", 1)
                        if l == 1:
                            v.wait("cv", c.val("cv"))
                            v.tensor_add(ttmp[:, 1024:2048], ttmp[:, 1024:2048],
                                         ttm2[:, 0:1024]).then_inc("cv", 1)
                        y.wait("cv", c.val("cv"))
                        dstt = out if l == 1 else xfs
                        y.dma_start(out=dstt[128 * ti:128 * (ti + 1), :],
                                    in_=ttmp[:, 1024:2048]).then_inc("ld", 16)
                        if dbg and l == 0:
                            y.dma_start(out=dbg_t["d_xf2"][128 * ti:128 * (ti + 1), :],
                                        in_=ttmp[:, 1024:2048]).then_inc("so", 16)
                        y.wait("ld", c.val("ld"))
                        if dbg:
                            y.wait("so", c.val("so"))
                if USE_PRELOAD:
                    preload_set("natural_log_exp_and_others")
                bar()

        c.start(None)
        phases(c)
        expect = dict(c.vals)
        blk = es.enter_context(nc.Block())
        for en in ["sync", "vector", "scalar", "tensor", "gpsimd"]:
            def run(eng, en=en):
                c.start(en)
                phases(c)
                assert c.vals == expect, (en, c.vals, expect)
            getattr(blk, en)(run)
    return nc, dbg_t


def _core_inputs(xi, tabs):
    d = {"x": np.ascontiguousarray(xi)}
    d.update(tabs)
    return d


def kernel(**inputs):
    x = np.asarray(inputs["L0_attn_x"])
    nc, _ = build_program(dbg=False)
    base = {
        "Wq": inputs["L0_attn_Wq"], "Wk": inputs["L0_attn_Wk"],
        "Wv": inputs["L0_attn_Wv"], "Wo": inputs["L0_attn_Wo"],
        "bq": inputs["L0_attn_bq"].reshape(1, D),
        "bk": inputs["L0_attn_bk"].reshape(1, D),
        "bv": inputs["L0_attn_bv"].reshape(1, D),
        "bo": inputs["L0_attn_bo"].reshape(1, D),
    }
    for l in range(2):
        base[f"wqkv{l}"] = inputs["L0_ffn_wqkv"][l]
        base[f"qsc{l}"] = inputs["L0_ffn_q_scale"][l].reshape(1, 64)
        base[f"ksc{l}"] = inputs["L0_ffn_k_scale"][l].reshape(1, 64)
        base[f"wol{l}"] = inputs["L0_ffn_wo"][l]
        base[f"lng{l}"] = inputs["L0_ffn_ln_g"][l].reshape(1, D)
        base[f"lnb{l}"] = inputs["L0_ffn_ln_b"][l].reshape(1, D)
        base[f"w1_{l}"] = inputs["L0_ffn_w1"][l]
        base[f"b1_{l}"] = inputs["L0_ffn_b1"][l].reshape(1, 2 * HID)
        base[f"w2_{l}"] = inputs["L0_ffn_w2"][l]
        base[f"b2_{l}"] = inputs["L0_ffn_b2"][l].reshape(1, D)
    for k, v_ in TABLES.items():
        base["tab_" + k] = v_
    in_maps = []
    for b in range(8):
        m = dict(base)
        m["x"] = np.ascontiguousarray(x[b])
        in_maps.append(m)
    res = run_bass_kernel_spmd(nc, in_maps, core_ids=list(range(8)))
    return np.stack([r["out"] for r in res.results], 0).astype(np.float32)
